# version 14
# baseline (speedup 1.0000x reference)
import numpy as np
from contextlib import ExitStack

import concourse.bass as bass
import concourse.mybir as mybir
from concourse.bass_utils import run_bass_kernel_spmd

F32 = mybir.dt.float32
BF16 = mybir.dt.bfloat16
AF = mybir.ActivationFunctionType
ALU = mybir.AluOpType
AX = mybir.AxisListType

MAGIC = 12582912.0
EPS = 1e-5
DM = 1024
HC = DM // 128
NH = 16
DH = 64
INNER = 4096
IC = INNER // 128
PST = 160
VST = HC * PST

ENGINES = ("sync", "gpsimd", "scalar", "vector", "tensor")


class Prog:

    def __init__(self, nc, es):
        self.nc = nc
        self.ops = {e: [] for e in ENGINES}
        self.cnt = {}
        self.h = {}
        self.lastw = {e: {} for e in ENGINES}
        for name in ("syn", "gps", "gpd", "ten", "vec", "sca"):
            self.h[name] = es.enter_context(nc.semaphore(name))
            self.cnt[name] = 0

    def v(self, name):
        return self.cnt[name]

    DMA_DEPTH = 3

    def do(self, eng, fn, wait=(), inc=None):
        if eng == "sync" and getattr(self, "route_sync_to_gpsimd", False):
            eng = "gpsimd"
        wait = list(wait)
        if inc is not None and inc[0] in ("syn", "gpd"):
            wait.append((inc[0], self.cnt[inc[0]] - 16 * (self.DMA_DEPTH - 1)))
        waits = []
        for (n, val) in wait:
            if val <= 0 or self.lastw[eng].get(n, -1) >= val:
                continue
            self.lastw[eng][n] = val
            waits.append((self.h[n], val))
        if inc is not None:
            iname, iamt = inc
            self.cnt[iname] += iamt
            ih = self.h[iname]
        else:
            ih, iamt = None, 0

        def run(e, waits=waits, fn=fn, ih=ih, iamt=iamt):
            for hh, val in waits:
                e.wait_ge(hh, val)
            ins = fn(e)
            if ih is not None:
                ins.then_inc(ih, iamt)

        self.ops[eng].append(run)

    def now(self, *names):
        return [(n, self.cnt[n]) for n in names]


class Arena:

    def __init__(self, nc, es, words, psum_banks=8):
        h = es.enter_context(nc.sbuf_tensor("arena", [128, words], F32))
        self.f = h[:, :]
        self.b = self.f.bitcast(BF16)
        self.words = words
        self.top = 0
        ph = es.enter_context(nc.psum_tensor("psarena", [128, psum_banks * 512], F32))
        self.pf = ph[:, :]
        self.ptop = 0
        self.pbanks = psum_banks

    def sb(self, name, shape, dt=F32):
        assert len(shape) == 2 and shape[0] == 128
        n = shape[1]
        words = n if dt == F32 else (n + 1) // 2
        off = (self.top + 15) & ~15
        self.top = off + words
        assert self.top <= self.words, f"SBUF arena overflow at {name}"
        if dt == F32:
            return self.f[:, off:off + n]
        return self.b[:, 2 * off:2 * off + n]

    def ps(self, name, shape):
        assert shape[0] == 128 and shape[1] % 512 == 0
        nb = shape[1] // 512
        off = self.ptop
        self.ptop = off + nb
        assert self.ptop <= self.pbanks, f"PSUM arena overflow at {name}"
        return self.pf[:, off * 512:(off + nb) * 512]

    def mark(self):
        return (self.top, self.ptop)

    def release(self, m):
        self.top, self.ptop = m


def build_nc(SEQ=4096, phases="ABC", dbg_out=False):
    assert SEQ % 1024 == 0
    OWN = SEQ // 2
    KC = SEQ // 128
    NTA = SEQ // 512
    NTQ = OWN // 512
    NSUB = OWN // 128
    KP = KC // 2
    NGRP = OWN // 512

    nc = bass.Bass("TRN2", target_bir_lowering=False, detect_race_conditions=False)
    nc.reset()

    xb_d = nc.dram_tensor("xb", [SEQ, DM], F32, kind="ExternalInput").ap()
    xq_d = nc.dram_tensor("xq", [OWN, DM], F32, kind="ExternalInput").ap()
    wq_d = nc.dram_tensor("wq", [DM, DM], F32, kind="ExternalInput").ap()
    wk_d = nc.dram_tensor("wk", [DM, DM], F32, kind="ExternalInput").ap()
    wv_d = nc.dram_tensor("wv", [DM, DM], F32, kind="ExternalInput").ap()
    wo_d = nc.dram_tensor("wo", [DM, DM], F32, kind="ExternalInput").ap()
    wg_d = nc.dram_tensor("wg", [INNER, DM], F32, kind="ExternalInput").ap()
    wu_d = nc.dram_tensor("wu", [INNER, DM], F32, kind="ExternalInput").ap()
    wd_d = nc.dram_tensor("wd", [DM, INNER], F32, kind="ExternalInput").ap()
    out_d = nc.dram_tensor("out", [OWN, DM], F32, kind="ExternalOutput").ap()

    K_I = "ExternalOutput"
    scr_K = nc.dram_tensor("scr_K", [128, HC * SEQ], BF16, kind=K_I).ap()
    dbg_q = nc.dram_tensor("dbg_q", [128, HC * OWN], BF16, kind=K_I).ap()
    dbg_k = nc.dram_tensor("dbg_k", [128, SEQ], BF16, kind=K_I).ap()
    dbg_e = nc.dram_tensor("dbg_e", [128, 4096], BF16, kind=K_I).ap()
    dbg_q2 = nc.dram_tensor("dbg_q2", [128, HC * OWN], BF16, kind=K_I).ap()
    dbg_q3 = nc.dram_tensor("dbg_q3", [128, HC * OWN], BF16, kind=K_I).ap()
    scr_x1 = nc.dram_tensor("scr_x1", [128, NSUB * DM], F32, kind=K_I).ap()
    scr_h1q = nc.dram_tensor("scr_h1q", [128, HC * OWN], BF16, kind=K_I).ap()
    scr_wgT = nc.dram_tensor("scr_wgT", [128, IC * DM], BF16, kind=K_I).ap()
    scr_wuT = nc.dram_tensor("scr_wuT", [128, IC * DM], BF16, kind=K_I).ap()
    scr_wdT = nc.dram_tensor("scr_wdT", [128, HC * INNER], BF16, kind=K_I).ap()

    es = ExitStack()
    ar = Arena(nc, es, 53200)
    sb = ar.sb
    p = Prog(nc, es)
    p.route_sync_to_gpsimd = False

    ident = sb("ident", [128, 128], BF16)
    ones128 = sb("ones128", [128, 128])
    epsc = sb("epsc", [128, 1])
    zpad = sb("zpad", [128, 1])
    onecol = sb("onecol", [128, 1])
    c1sc = sb("c1sc", [128, NSUB])
    cdwsc = sb("cdwsc", [128, NSUB])
    wsc = sb("wsc", [128, 4])
    swb = sb("swb", [128, 1])
    st_sum = sb("st_sum", [128, 4])
    st_sq = sb("st_sq", [128, 4])
    st_mean = sb("st_mean", [128, 4])
    st_var = sb("st_var", [128, 4])
    st_rstd = sb("st_rstd", [128, 4])
    st_t = sb("st_t", [128, 4])
    st_t2 = sb("st_t2", [128, 4])
    st_a = sb("st_a", [128, 4])
    st_b = sb("st_b", [128, 4])
    sqd = sb("sqd", [128, DM], BF16)

    p.do("gpsimd", lambda g: g.memset(ident[:, :], 0.0), inc=("gps", 1))
    p.do("gpsimd", lambda g: g.affine_select(
        out=ident[:, :], in_=ident[:, :], compare_op=ALU.not_equal,
        fill=1.0, base=0, pattern=[[-1, 128]], channel_multiplier=1),
        wait=p.now("gps"), inc=("gps", 1))
    p.do("gpsimd", lambda g: g.memset(ones128[:, :], 1.0), inc=("gps", 1))
    p.do("gpsimd", lambda g: g.memset(epsc[:, :], EPS), inc=("gps", 1))
    p.do("gpsimd", lambda g: g.memset(zpad[:, :], 0.0), inc=("gps", 1))
    p.do("gpsimd", lambda g: g.memset(onecol[:, :], 1.0), inc=("gps", 1))

    settle = sb("settle", [128, 16])
    for i in range(12):
        p.do("sync", lambda s, i=i: s.dma_start(
            out=settle[:, :], in_=xb_d[0:128, i * 16:(i + 1) * 16]),
            wait=p.now("syn"), inc=("syn", 16))
    for i in range(4):
        p.do("gpsimd", lambda g, i=i: g.dma_start(
            out=settle[:, :], in_=xq_d[0:128, i * 16:(i + 1) * 16]),
            wait=p.now("gpd"), inc=("gpd", 16))
    _settled = [("syn", p.v("syn")), ("gpd", p.v("gpd"))]
    p.do("vector", lambda v: v.nop(), wait=_settled, inc=("vec", 1))
    p.do("scalar", lambda s: s.nop(), wait=_settled, inc=("sca", 1))
    p.do("tensor", lambda t: t.nop(), wait=_settled, inc=("ten", 1))

    p.do("vector", lambda v: v.memset(settle[:, 0:1], 1.0), inc=("vec", 1))
    _pace0 = p.v("vec")

    def emit_pace(n=1):
        for _ in range(n):
            def tick(v):
                for _ in range(60):
                    v.tensor_scalar(settle[:, 0:1], settle[:, 0:1], 1.0, None,
                                    ALU.mult)
                return v.nop()
            p.do("vector", tick, inc=("vec", 1))

    mark_AB = ar.mark()
    sbAB = ar.sb
    vaug = sbAB("vaug", [128, KC * VST], BF16)
    qfm = sbAB("qfm", [128, HC * OWN], BF16)

    p.do("gpsimd", lambda g: g.memset(vaug[:, :], 0.0), inc=("gps", 1))
    p.do("gpsimd", lambda g: g.memset(
        vaug[:, :].rearrange("p (k j c) -> p k j c", k=KC, j=HC)[:, :, :, 64:65],
        1.0), wait=p.now("gps"), inc=("gps", 1))
    gps_const = p.v("gps")

    def emit_ln_stats(xin, nsub, xwait):
        for s_i in range(nsub):
            xs = xin[:, s_i * DM:(s_i + 1) * DM]
            p.do("vector", lambda v, xs=xs, s_i=s_i: v.tensor_reduce(
                st_sum[:, s_i:s_i + 1], xs, axis=AX.X, op=ALU.add),
                wait=xwait, inc=("vec", 1))
            p.do("scalar", lambda s, xs=xs, s_i=s_i: s.activation(
                sqd[:, :], xs, AF.Square, accum_out=st_sq[:, s_i:s_i + 1]),
                wait=list(xwait) + [("sca", p.v("sca"))], inc=("sca", 1))
        p.do("vector", lambda v, nsub=nsub: v.tensor_scalar(
            st_mean[:, 0:nsub], st_sum[:, 0:nsub], 1.0 / DM, None, ALU.mult),
            wait=p.now("sca"), inc=("vec", 1))
        p.do("vector", lambda v, nsub=nsub: v.tensor_tensor(
            st_t[:, 0:nsub], st_mean[:, 0:nsub], st_mean[:, 0:nsub], op=ALU.mult),
            wait=p.now("vec"), inc=("vec", 1))
        p.do("vector", lambda v, nsub=nsub: v.scalar_tensor_tensor(
            st_var[:, 0:nsub], st_sq[:, 0:nsub], 1.0 / DM, st_t[:, 0:nsub],
            op0=ALU.mult, op1=ALU.subtract), wait=p.now("vec"), inc=("vec", 1))
        p.do("scalar", lambda s, nsub=nsub: s.activation(
            st_rstd[:, 0:nsub], st_var[:, 0:nsub], AF.Sqrt, bias=epsc[:, :],
            scale=1.0), wait=p.now("vec") + [("gps", gps_const)], inc=("sca", 1))
        p.do("vector", lambda v, nsub=nsub: v.reciprocal(
            st_rstd[:, 0:nsub], st_rstd[:, 0:nsub]),
            wait=p.now("sca"), inc=("vec", 1))

    mark_A = ar.mark()
    sbA = ar.sb
    w1_sb = sbA("w1_sb", [128, HC * DM], BF16)
    w2_sb = sbA("w2_sb", [128, HC * DM], BF16)
    xta = sbA("xta", [128, 4 * DM])
    lntm = sbA("lntm", [128, 4 * DM], BF16)
    lnt = sbA("lnt", [128, 2 * 4096], BF16)
    ksp = sbA("ksp", [128, HC * 512], BF16)
    nbias = sbA("nbias", [128, 4])
    pA_tr = ar.ps("pA_tr", [128, 1024])
    pA_o = ar.ps("pA_o", [128, 1024])
    pA_v = ar.ps("pA_v", [128, 1024])

    def wload(dst, src, wait=(), paced=False):
        for c in range(HC):
            if paced:
                emit_pace(1)
                wait = list(wait) + p.now("vec")
            p.do("gpsimd", lambda g, dst=dst, src=src, c=c: g.dma_start(
                out=dst[:, c * DM:(c + 1) * DM],
                in_=src[c * 128:(c + 1) * 128, :]),
                wait=wait, inc=("gpd", 16))

    wload(w1_sb, wk_d, paced=True)
    wload(w2_sb, wv_d, paced=True)
    wkv_ld = p.v("gpd")
    emit_pace(8)


    lnt_consumed = [0, 0]
    stateA = {"norm": 0, "tr_evac": 0, "o_evac": [0, 0], "v_evac": 0,
              "k_spill": 0, "lntm_rd": 0}

    def phaseA_tile(t_i, x_src, row0, do_kv, do_q, qcol0, w_ld):
        buf = t_i % 2
        lt3 = lnt[:, buf * 4096:(buf + 1) * 4096].rearrange(
            "p (h f) -> p h f", h=HC)
        for a in range(4):
            p.do("sync", lambda s, a=a: s.dma_start(
                out=xta[:, a * DM:(a + 1) * DM],
                in_=x_src[row0 + a * 128:row0 + (a + 1) * 128, :]),
                wait=[("sca", stateA["norm"]), ("vec", p.v("vec"))],
                inc=("syn", 16))
        if t_i < 2:
            emit_pace(60)
        xw = [("syn", p.v("syn")), ("vec", p.v("vec"))]
        emit_ln_stats(xta, 4, xw)
        p.do("vector", lambda v: v.scalar_tensor_tensor(
            nbias[:, 0:4], st_mean[:, 0:4], -1.0, st_rstd[:, 0:4],
            op0=ALU.mult, op1=ALU.mult), wait=p.now("vec"), inc=("vec", 1))
        for s_i in range(4):
            p.do("scalar", lambda s, s_i=s_i: s.activation(
                lntm[:, s_i * DM:(s_i + 1) * DM], xta[:, s_i * DM:(s_i + 1) * DM],
                AF.Identity, bias=nbias[:, s_i:s_i + 1],
                scale=st_rstd[:, s_i:s_i + 1]),
                wait=p.now("vec") + [("ten", stateA["lntm_rd"])], inc=("sca", 1))
        stateA["norm"] = p.v("sca")
        if "E" in phases and t_i == 0:
            p.do("sync", lambda s: s.dma_start(out=dbg_e[:, :], in_=lntm[:, :]),
                 wait=p.now("sca"), inc=("syn", 16))
            p.do("sync", lambda s: s.dma_start(
                out=dbg_k[:, 0:8].bitcast(F32),
                in_=st_rstd[:, 0:4], allow_slow_non_contiguous=True), inc=("syn", 16))
            p.do("sync", lambda s: s.dma_start(
                out=dbg_k[:, 16:24].bitcast(F32),
                in_=st_mean[:, 0:4], allow_slow_non_contiguous=True), inc=("syn", 16))
            _dcnt = p.v("syn")
            p.do("vector", lambda v: v.nop(), wait=[("syn", _dcnt)], inc=("vec", 1))
            p.do("scalar", lambda s: s.nop(), wait=[("syn", _dcnt)], inc=("sca", 1))
            p.do("tensor", lambda t: t.nop(), wait=[("syn", _dcnt)], inc=("ten", 1))
        for s_i in range(4):
            for hcb in range(HC):
                p.do("tensor", lambda t, s_i=s_i, hcb=hcb: t.matmul(
                    pA_tr[:, hcb * 128:(hcb + 1) * 128],
                    lntm[:, s_i * DM + hcb * 128: s_i * DM + (hcb + 1) * 128],
                    ident[:, :], start=True, stop=True),
                    wait=[("sca", stateA["norm"]),
                          ("vec", stateA["tr_evac"]),
                          ("gps", gps_const),
                          ("ten", lnt_consumed[buf] if (s_i == 0 and hcb == 0) else 0)],
                    inc=("ten", 1) if hcb == HC - 1 else None)
            p.do("vector", lambda v, s_i=s_i, lt3=lt3: v.tensor_copy(
                lt3[:, :, s_i * 128:(s_i + 1) * 128],
                pA_tr[:, :].rearrange("p (h f) -> p h f", h=HC)),
                wait=p.now("ten"), inc=("vec", 1))
            stateA["tr_evac"] = p.v("vec")
        stateA["lntm_rd"] = p.v("ten")
        lnt_ready = p.v("vec")
        if "E" in phases and t_i == 0 and do_kv:
            p.do("sync", lambda s: s.dma_start(out=dbg_q2[:, 0:HC*DM],
                 in_=w1_sb[:, :]), wait=p.now("vec"), inc=("syn", 16))
            p.do("sync", lambda s, lt3=lt3: s.dma_start(
                 out=dbg_q3[:, 0:4096],
                 in_=lnt[:, (t_i % 2) * 4096:(t_i % 2) * 4096 + 4096]),
                 inc=("syn", 16))
            _d2 = p.v("syn")
            p.do("vector", lambda v: v.nop(), wait=[("syn", _d2)], inc=("vec", 1))
            p.do("scalar", lambda s: s.nop(), wait=[("syn", _d2)], inc=("sca", 1))
            p.do("tensor", lambda t: t.nop(), wait=[("syn", _d2)], inc=("ten", 1))
        if do_kv:
            if t_i < 2:
                emit_pace(40)
                p.do("tensor", lambda t: t.nop(), wait=p.now("vec"),
                     inc=("ten", 1))
            for fc in range(HC):
                par = fc % 2
                pk = pA_o[:, par * 512:par * 512 + 512]
                for hcb in range(HC):
                    p.do("tensor", lambda t, fc=fc, hcb=hcb, pk=pk, lt3=lt3: t.matmul(
                        pk, w1_sb[:, hcb * DM + fc * 128: hcb * DM + (fc + 1) * 128],
                        lt3[:, hcb, :], start=(hcb == 0), stop=(hcb == HC - 1)),
                        wait=[("vec", lnt_ready), ("gpd", w_ld),
                              ("vec", stateA["o_evac"][par] if hcb == 0 else 0)],
                        inc=("ten", 1) if hcb == HC - 1 else None)
                p.do("vector", lambda v, fc=fc, pk=pk: v.tensor_copy(
                    ksp[:, fc * 512:(fc + 1) * 512], pk),
                    wait=p.now("ten") + [("syn", stateA["k_spill"])],
                    inc=("vec", 1))
                stateA["o_evac"][par] = p.v("vec")
            for fc in range(HC):
                p.do("sync", lambda s, row0=row0, fc=fc: s.dma_start(
                    out=scr_K[:, fc * SEQ + row0: fc * SEQ + row0 + 512],
                    in_=ksp[:, fc * 512:(fc + 1) * 512]),
                    wait=p.now("vec"), inc=("syn", 16))
            stateA["k_spill"] = p.v("syn")
            for s_i in range(4):
                kc = row0 // 128 + s_i
                for hcb in range(HC):
                    for nh in range(2):
                        pv = pA_v[:, nh * 512:nh * 512 + 512]
                        p.do("tensor", lambda t, s_i=s_i, hcb=hcb, nh=nh, pv=pv,
                             lt3=lt3: t.matmul(
                            pv, lt3[:, hcb, s_i * 128:(s_i + 1) * 128],
                            w2_sb[:, hcb * DM + nh * 512: hcb * DM + (nh + 1) * 512],
                            start=(hcb == 0), stop=(hcb == HC - 1)),
                            wait=[("vec", lnt_ready),
                                  ("vec", stateA["v_evac"] if (hcb == 0 and nh == 0) else 0)],
                            inc=("ten", 1) if (hcb == HC - 1 and nh == 1) else None)
                p.do("vector", lambda v, kc=kc: v.tensor_copy(
                    vaug[:, kc * VST: kc * VST + VST].rearrange(
                        "p (j c) -> p j c", j=HC)[:, :, 0:64],
                    pA_v[:, :].rearrange("p (j c) -> p j c", j=HC)[:, :, 0:64]),
                    wait=p.now("ten") + [("gps", gps_const)], inc=("vec", 1))
                p.do("vector", lambda v, kc=kc: v.tensor_copy(
                    vaug[:, kc * VST: kc * VST + VST].rearrange(
                        "p (j c) -> p j c", j=HC)[:, :, 96:160],
                    pA_v[:, :].rearrange("p (j c) -> p j c", j=HC)[:, :, 64:128]),
                    inc=("vec", 1))
                stateA["v_evac"] = p.v("vec")
        if do_q:
            for fc in range(HC):
                par = fc % 2
                pk = pA_o[:, par * 512:par * 512 + 512]
                for hcb in range(HC):
                    p.do("tensor", lambda t, fc=fc, hcb=hcb, pk=pk, lt3=lt3: t.matmul(
                        pk, w1_sb[:, hcb * DM + fc * 128: hcb * DM + (fc + 1) * 128],
                        lt3[:, hcb, :], start=(hcb == 0), stop=(hcb == HC - 1)),
                        wait=[("vec", lnt_ready), ("gpd", w_ld),
                              ("vec", stateA["o_evac"][par] if hcb == 0 else 0)],
                        inc=("ten", 1) if hcb == HC - 1 else None)
                p.do("vector", lambda v, fc=fc, pk=pk, qcol0=qcol0: v.tensor_copy(
                    qfm[:, fc * OWN + qcol0: fc * OWN + qcol0 + 512], pk),
                    wait=p.now("ten"), inc=("vec", 1))
                stateA["o_evac"][par] = p.v("vec")
        lnt_consumed[buf] = p.v("ten")

    phaseA_tile(0, xb_d, 0, True, False, 0, wkv_ld)
    phaseA_tile(1, xb_d, 512, True, False, 0, wkv_ld)
    for t_i in range(NTA):
        phaseA_tile(t_i, xb_d, t_i * 512, True, False, 0, wkv_ld)
    wload(w1_sb, wq_d, wait=p.now("ten"))
    wq_ld = p.v("gpd")
    for t_i in range(NTQ):
        phaseA_tile(NTA + t_i, xq_d, t_i * 512, False, True, t_i * 512, wq_ld)

    ar.release(mark_A)

    if "B" not in phases:
        if "D" in phases:
            p.do("sync", lambda s: s.dma_start(out=dbg_q[:, :], in_=qfm[:, :]),
                 wait=p.now("vec"), inc=("syn", 16))
            for a in range(4):
                p.do("sync", lambda s, a=a: s.dma_start(
                    out=xta[:, a * DM:(a + 1) * DM],
                    in_=xb_d[a * 128:(a + 1) * 128, :]),
                    wait=p.now("vec", "ten", "sca"), inc=("syn", 16))
            p.do("sync", lambda s: s.dma_start(out=dbg_e[:, 0:4096].bitcast(F32)[:, 0:2048],
                 in_=xta[:, 0:2048]), inc=("syn", 16))
            p.do("sync", lambda s: s.dma_start(out=dbg_k[:, 0:2048].bitcast(F32)[:, 0:1024],
                 in_=xta[:, 2048:3072]), inc=("syn", 16))
        fin0 = p.v("syn")
        p.do("sync", lambda s: s.nop(),
             wait=[("syn", fin0), ("ten", p.v("ten")), ("vec", p.v("vec")),
                   ("sca", p.v("sca")), ("gpd", p.v("gpd"))])
        return _emit(nc, p, es)
    mark_B = ar.mark()
    sbB = ar.sb
    psB = ar.ps

    ksl = sbB("ksl", [128, 2 * SEQ], BF16)
    wo_sb = sbB("wo_sb", [128, HC * DM], BF16)
    expt = sbB("expt", [128, 2 * 2048], BF16)
    zrow = sbB("zrow", [128, 1024])
    attn = sbB("attn", [128, HC * 512], BF16)
    ao_sb = sbB("ao_sb", [128, HC * 512], BF16)
    ao_tm = sbB("ao_tm", [128, 4 * DM], BF16)
    xqt = sbB("xqt", [128, 2 * DM])
    x1t = sbB("x1t", [128, DM])
    f32x = sbB("f32x", [128, DM])
    h1qst = sbB("h1qst", [128, DM], BF16)
    wraw = sbB("wraw", [128, 1024])
    wpos = sqd[:, 0:512]
    wneg = h1qst[:, 512:1024]
    wqn = sbB("wqn", [128, 512], BF16)
    wqt = h1qst[:, 0:512]
    r8 = sbB("r8", [128, 66])

    psS_A = psB("psS_A", [128, 1024])
    psS_B = psB("psS_B", [128, 1024])
    psAV_A = psB("psAV_A", [128, 512])
    psAV_B = psB("psAV_B", [128, 512])
    psMSC = psB("psMSC", [128, 1024])
    psPP = psMSC[:, 0:512]

    a_end = p.now("ten", "vec", "sca")
    p.do("gpsimd", lambda g: g.memset(zrow[:, :], 0.0), wait=a_end, inc=("gps", 1))
    wload(wo_sb, wo_d, wait=a_end)
    wo_ld = p.v("gpd")
    if "D" in phases:
        p.do("sync", lambda s: s.dma_start(out=dbg_q2[:, :], in_=qfm[:, :]),
             wait=p.now("vec"), inc=("syn", 16))
    p.do("tensor", lambda t: t.nop(), wait=p.now("vec", "gps"), inc=("ten", 1))
    p.do("scalar", lambda s: s.nop(), wait=p.now("vec", "gps"), inc=("sca", 1))

    def make_prepass():
        mats = [(wg_d, scr_wgT, 0, False), (wu_d, scr_wuT, 1, False),
                (wd_d, scr_wdT, 2, True)]
        for (src, dst, col, iswd) in mats:
            nj = 8 if iswd else 2
            src4 = src.rearrange("(c p) (j f) -> c j p f", p=128, f=512)
            src3 = [src4[i // nj, i % nj] for i in range(64)]
            for piece in range(65):
                def absstep(piece=piece, src3=src3):
                    if piece < 64:
                        buf = wraw[:, (piece % 2) * 512:(piece % 2) * 512 + 512]
                        p.do("sync", lambda s, buf=buf, piece=piece: s.dma_start(
                            out=buf, in_=src3[piece]),
                            wait=p.now("vec"), inc=("syn", 16))
                    if piece > 0:
                        pv = piece - 1
                        buf = wraw[:, (pv % 2) * 512:(pv % 2) * 512 + 512]
                        p.do("vector", lambda v, pv=pv, buf=buf: v.tensor_reduce(
                            r8[:, pv:pv + 1], buf, axis=AX.X, op=ALU.add,
                            apply_absolute_value=True),
                            wait=p.now("syn"), inc=("vec", 1))
                yield absstep
            def meanstep(col=col):
                p.do("vector", lambda v: v.tensor_reduce(
                    r8[:, 64:65], r8[:, 0:64], axis=AX.X, op=ALU.add),
                    inc=("vec", 1))
                p.do("tensor", lambda t: t.matmul(
                    psPP[0:1, 0:1], r8[:, 64:65], onecol[:, :],
                    start=True, stop=True),
                    wait=p.now("vec"), inc=("ten", 1))
                p.do("vector", lambda v: v.tensor_scalar(
                    zpad[0:1, 0:1], psPP[0:1, 0:1], 1.0 / float(INNER * DM),
                    EPS, ALU.mult, ALU.max),
                    wait=p.now("ten"), inc=("vec", 1))
                p.do("tensor", lambda t: t.matmul(
                    psPP[:, 0:1], ones128[:, :], zpad[:, 0:1],
                    start=True, stop=True),
                    wait=p.now("vec"), inc=("ten", 1))
                p.do("vector", lambda v, col=col: v.tensor_copy(
                    wsc[:, col:col + 1], psPP[:, 0:1]), wait=p.now("ten"),
                    inc=("vec", 1))
                p.do("vector", lambda v, col=col: v.reciprocal(
                    swb[:, :], wsc[:, col:col + 1]), wait=p.now("vec"),
                    inc=("vec", 1))
            yield meanstep
            for piece in range(65):
                def qtstep(piece=piece, src3=src3, dst=dst):
                    if piece < 64:
                        buf = wraw[:, (piece % 2) * 512:(piece % 2) * 512 + 512]
                        p.do("sync", lambda s, buf=buf, piece=piece: s.dma_start(
                            out=buf, in_=src3[piece]),
                            wait=p.now("vec"), inc=("syn", 16))
                    if piece == 0:
                        return
                    piece = piece - 1
                    buf = wraw[:, (piece % 2) * 512:(piece % 2) * 512 + 512]
                    p.do("vector", lambda v, buf=buf: v.tensor_scalar(
                        wpos[:, :], buf, swb[:, :], 0.5, ALU.mult,
                        ALU.is_ge),
                        wait=p.now("syn") + p.now("gpd") + p.now("vec"))
                    p.do("vector", lambda v, buf=buf: v.tensor_scalar(
                        wneg[:, :], buf, swb[:, :], -0.5, ALU.mult,
                        ALU.is_le))
                    p.do("vector", lambda v: v.tensor_tensor(
                        wqn[:, :], wpos[:, :], wneg[:, :], op=ALU.subtract),
                        inc=("vec", 1), wait=p.now("ten"))
                    for b in range(4):
                        p.do("tensor", lambda t, b=b: t.matmul(
                            psPP[:, b * 128:(b + 1) * 128],
                            wqn[:, b * 128:(b + 1) * 128], ident[:, :],
                            start=True, stop=True),
                            wait=p.now("vec"),
                            inc=("ten", 1) if b == 3 else None)
                    p.do("vector", lambda v: v.tensor_copy(
                        wqt[:, :], psPP[:, :]),
                        wait=p.now("ten") + p.now("gpd"), inc=("vec", 1))
                    p.do("gpsimd", lambda g, dst=dst, piece=piece: g.dma_start(
                        out=dst[:, piece * 512:(piece + 1) * 512], in_=wqt[:, :]),
                        wait=p.now("vec"), inc=("gpd", 16))
                yield qtstep

    pp_list = list(make_prepass())
    pp_idx = [0]

    def run_prepass_steps(n):
        while n > 0 and pp_idx[0] < len(pp_list):
            pp_list[pp_idx[0]]()
            pp_idx[0] += 1
            n -= 1

    prev_exp = {0: 0, 1: 0}
    prev_av = [0, 0]
    av_evac_prev = 0
    wout_done = 0

    for qt in range(NTQ):
        q0 = qt * 512
        for hp in range(HC):
            kbuf = hp % 2
            kb = ksl[:, kbuf * SEQ:(kbuf + 1) * SEQ]
            ksl_guard = prev_exp[0]
            p.do("sync", lambda s, hp=hp, kb=kb: s.dma_start(
                out=kb, in_=scr_K[:, hp * SEQ:(hp + 1) * SEQ]),
                wait=[("sca", ksl_guard)] + list(a_end), inc=("syn", 16))
            syn_k = p.v("syn")
            if "D" in phases and qt == 0 and hp == 0:
                p.do("sync", lambda s, kb=kb: s.dma_start(out=dbg_k[:, :], in_=kb),
                     wait=p.now("vec"), inc=("syn", 16))
                p.do("sync", lambda s: s.dma_start(out=dbg_q[:, :], in_=qfm[:, :]),
                     inc=("syn", 16))
                syn_k = p.v("syn")
            exp_cnt = {}
            av_cnt = {}
            for kp in range(KP):
                kc0 = 2 * kp
                slot = kp % 2
                for half in range(2):
                    kcx = kc0 + half
                    p.do("tensor", lambda t, kb=kb, kcx=kcx, half=half, q0=q0,
                         hp=hp: t.matmul(
                        psS_A[:, half * 512:(half + 1) * 512],
                        kb[0:64, kcx * 128:(kcx + 1) * 128],
                        qfm[0:64, hp * OWN + q0: hp * OWN + q0 + 512],
                        start=True, stop=True, tile_position=(0, 0)),
                        wait=[("syn", syn_k),
                              ("sca", exp_cnt.get((kp - 1, 0), prev_exp[0]))])
                    p.do("tensor", lambda t, kb=kb, kcx=kcx, half=half, q0=q0,
                         hp=hp: t.matmul(
                        psS_B[:, half * 512:(half + 1) * 512],
                        kb[64:128, kcx * 128:(kcx + 1) * 128],
                        qfm[64:128, hp * OWN + q0: hp * OWN + q0 + 512],
                        start=True, stop=True, tile_position=(64, 0)),
                        wait=[("sca", exp_cnt.get((kp - 1, 1), prev_exp[1]))],
                        inc=("ten", 1) if half == 1 else None)
                sc_done = p.v("ten")
                for head, srcp in ((0, psS_A), (1, psS_B)):
                    p.do("scalar", lambda s, head=head, srcp=srcp, slot=slot:
                         s.activation(
                             expt[:, head * 2048 + slot * 1024:
                                  head * 2048 + slot * 1024 + 1024],
                             srcp[:, :], AF.Exp, scale=0.125),
                         wait=[("ten", sc_done),
                               ("ten", av_cnt.get(kp - 2, prev_av[slot]))],
                         inc=("sca", 1))
                    exp_cnt[(kp, head)] = p.v("sca")
                if "D" in phases and qt == 0 and hp == 0 and kp == 0:
                    p.do("sync", lambda s: s.dma_start(out=dbg_e[:, :], in_=expt[:, :]),
                         wait=[("sca", exp_cnt[(kp, 1)])], inc=("syn", 16))
                for (j, kcx) in ((0, kc0), (1, kc0 + 1)):
                    for head, pav in ((0, psAV_A), (1, psAV_B)):
                        base = kcx * VST + hp * PST
                        if head == 0:
                            lhs = vaug[:, base:base + 65]
                            outp = pav[0:65, :]
                        else:
                            lhs = vaug[:, base + 32:base + 160]
                            outp = pav[0:128, :]
                        p.do("tensor", lambda t, lhs=lhs, outp=outp, head=head,
                             slot=slot, j=j, kcx=kcx: t.matmul(
                            outp, lhs,
                            expt[:, head * 2048 + slot * 1024 + j * 512:
                                 head * 2048 + slot * 1024 + (j + 1) * 512],
                            start=(kcx == 0), stop=(kcx == KC - 1)),
                            wait=[("sca", exp_cnt[(kp, head)]),
                                  ("vec", av_evac_prev if kcx == 0 else 0)],
                            inc=("ten", 1) if (head == 1 and j == 1) else None)
                    av_cnt[kp] = p.v("ten")
            av_done = p.v("ten")
            prev_exp[0] = exp_cnt[(KP - 1, 0)]
            prev_exp[1] = exp_cnt[(KP - 1, 1)]
            prev_av = [av_cnt.get(KP - 2, av_done), av_cnt[KP - 1]]
            p.do("scalar", lambda s: s.activation(
                f32x[64:65, 0:512], psAV_A[64:65, :], AF.Ln),
                wait=[("ten", av_done), ("sca", p.v("sca"))]
                + (p.now("vec") if hp == 0 else []), inc=("sca", 1))
            p.do("scalar", lambda s: s.activation(
                zrow[64:65, 0:512], f32x[64:65, 0:512], AF.Exp, scale=-1.0),
                wait=[("sca", p.v("sca"))], inc=("sca", 1))
            p.do("scalar", lambda s: s.activation(
                f32x[32:33, 0:512], psAV_B[32:33, :], AF.Ln),
                wait=[("ten", av_done), ("sca", p.v("sca"))], inc=("sca", 1))
            p.do("scalar", lambda s: s.activation(
                zrow[32:33, 512:1024], f32x[32:33, 0:512], AF.Exp, scale=-1.0),
                wait=[("sca", p.v("sca"))], inc=("sca", 1))
            rec_done = p.v("sca")
            p.do("tensor", lambda t: t.matmul(
                psMSC[:, 0:512], ones128[:, :], zrow[:, 0:512],
                start=True, stop=True),
                wait=[("sca", rec_done)] + p.now("vec"), inc=("ten", 1))
            p.do("tensor", lambda t: t.matmul(
                psMSC[:, 512:1024], ones128[:, :], zrow[:, 512:1024],
                start=True, stop=True), inc=("ten", 1))
            bc_done = p.v("ten")
            p.do("vector", lambda v, hp=hp: v.tensor_copy(
                attn[0:64, hp * 512:(hp + 1) * 512], psAV_A[0:64, :]),
                wait=[("ten", bc_done), ("ten", wout_done)], inc=("vec", 1))
            p.do("vector", lambda v, hp=hp: v.tensor_tensor(
                attn[0:64, hp * 512:(hp + 1) * 512],
                attn[0:64, hp * 512:(hp + 1) * 512],
                psMSC[0:64, 0:512], op=ALU.mult), inc=("vec", 1))
            p.do("vector", lambda v, hp=hp: v.tensor_copy(
                attn[64:128, hp * 512:(hp + 1) * 512], psAV_B[64:128, :]),
                inc=("vec", 1))
            p.do("vector", lambda v, hp=hp: v.tensor_tensor(
                attn[64:128, hp * 512:(hp + 1) * 512],
                attn[64:128, hp * 512:(hp + 1) * 512],
                psMSC[64:128, 512:1024], op=ALU.mult), inc=("vec", 1))
            av_evac_prev = p.v("vec")
            run_prepass_steps(2)

        if "D" in phases and qt == 0:
            p.do("sync", lambda s: s.dma_start(out=dbg_q3[:, 0:HC*512], in_=attn[:, :]),
                 wait=p.now("vec", "sca"), inc=("syn", 16))
        attn_ready = p.v("vec")
        wo_evac = [0, 0]
        for oc in range(HC):
            par = oc % 2
            pw = psMSC[:, par * 512:par * 512 + 512]
            for hcb in range(HC):
                p.do("tensor", lambda t, oc=oc, hcb=hcb, pw=pw: t.matmul(
                    pw, wo_sb[:, hcb * DM + oc * 128: hcb * DM + (oc + 1) * 128],
                    attn[:, hcb * 512:(hcb + 1) * 512],
                    start=(hcb == 0), stop=(hcb == HC - 1)),
                    wait=[("vec", attn_ready), ("gpd", wo_ld),
                          ("vec", wo_evac[par] if hcb == 0 else 0)],
                    inc=("ten", 1) if hcb == HC - 1 else None)
            p.do("vector", lambda v, oc=oc, pw=pw: v.tensor_copy(
                ao_sb[:, oc * 512:(oc + 1) * 512], pw),
                wait=p.now("ten"), inc=("vec", 1))
            wo_evac[par] = p.v("vec")
        wout_done = p.v("ten")
        tr_evac = 0
        for s_i in range(4):
            for oc in range(HC):
                p.do("tensor", lambda t, s_i=s_i, oc=oc: t.matmul(
                    psMSC[:, oc * 128:(oc + 1) * 128],
                    ao_sb[:, oc * 512 + s_i * 128: oc * 512 + (s_i + 1) * 128],
                    ident[:, :], start=True, stop=True),
                    wait=[("vec", p.v("vec") if (s_i == 0 and oc == 0) else tr_evac)],
                    inc=("ten", 1) if oc == HC - 1 else None)
            p.do("vector", lambda v, s_i=s_i: v.tensor_copy(
                ao_tm[:, s_i * DM:(s_i + 1) * DM], psMSC[:, :]),
                wait=p.now("ten"), inc=("vec", 1))
            tr_evac = p.v("vec")
            xs = ao_tm[:, s_i * DM:(s_i + 1) * DM]
            p.do("vector", lambda v, xs=xs, s_i=s_i: v.tensor_reduce(
                st_sum[:, s_i:s_i + 1], xs, axis=AX.X, op=ALU.add))
            p.do("vector", lambda v, xs=xs: v.tensor_tensor(
                f32x[:, :], xs, xs, op=ALU.mult), wait=p.now("sca"))
            p.do("vector", lambda v, s_i=s_i: v.tensor_reduce(
                st_sq[:, s_i:s_i + 1], f32x[:, :], axis=AX.X, op=ALU.add),
                inc=("vec", 1))
        p.do("vector", lambda v: v.tensor_scalar(
            st_mean[:, 0:4], st_sum[:, 0:4], 1.0 / DM, None, ALU.mult),
            inc=("vec", 1))
        p.do("vector", lambda v: v.tensor_tensor(
            st_t[:, 0:4], st_mean[:, 0:4], st_mean[:, 0:4], op=ALU.mult),
            wait=p.now("vec"), inc=("vec", 1))
        p.do("vector", lambda v: v.scalar_tensor_tensor(
            st_var[:, 0:4], st_sq[:, 0:4], 1.0 / DM, st_t[:, 0:4],
            op0=ALU.mult, op1=ALU.subtract), wait=p.now("vec"), inc=("vec", 1))
        p.do("scalar", lambda s: s.activation(
            st_rstd[:, 0:4], st_var[:, 0:4], AF.Sqrt, bias=epsc[:, :], scale=1.0),
            wait=p.now("vec"), inc=("sca", 1))
        p.do("vector", lambda v: v.reciprocal(st_rstd[:, 0:4], st_rstd[:, 0:4]),
             wait=p.now("sca"), inc=("vec", 1))
        p.do("vector", lambda v: v.scalar_tensor_tensor(
            st_b[:, 0:4], st_mean[:, 0:4], -1.0, st_rstd[:, 0:4],
            op0=ALU.mult, op1=ALU.mult), wait=p.now("vec"), inc=("vec", 1))
        if "F" in phases and qt == 0:
            dq3f = dbg_q3.bitcast(F32)
            p.do("sync", lambda s: s.dma_start(
                out=dbg_q3[:, 4096:8192], in_=ao_sb[:, :]),
                wait=p.now("vec", "ten", "sca"), inc=("syn", 16))
            p.do("sync", lambda s: s.dma_start(
                out=dbg_q3[:, 8192:12288], in_=ao_tm[:, :]), inc=("syn", 16))
            p.do("sync", lambda s: s.dma_start(
                out=dq3f[:, 0:4], in_=st_mean[:, 0:4]), inc=("syn", 16))
            p.do("sync", lambda s: s.dma_start(
                out=dq3f[:, 4:8], in_=st_rstd[:, 0:4]), inc=("syn", 16))
            p.do("sync", lambda s: s.dma_start(
                out=dq3f[:, 8:12], in_=st_sq[:, 0:4]), inc=("syn", 16))
            p.do("sync", lambda s: s.dma_start(
                out=dq3f[:, 12:16], in_=st_var[:, 0:4]), inc=("syn", 16))
            p.do("sync", lambda s: s.dma_start(
                out=dq3f[:, 16:20], in_=st_sum[:, 0:4]), inc=("syn", 16))
            p.do("sync", lambda s: s.dma_start(
                out=dq3f[:, 20:24], in_=st_t[:, 0:4]), inc=("syn", 16))
            p.do("sync", lambda s: s.dma_start(
                out=dq3f[:, 1024:2048], in_=f32x[:, :]), inc=("syn", 16))
        x1_spill = 0
        h1_spill = 0
        h1tr_prev = [0]
        p.do("sync", lambda s, qt=qt: s.dma_start(
            out=xqt[:, 0:DM].unsqueeze(1),
            in_=xq_d[qt * 512:qt * 512 + 128, :].rearrange(
                "(a p) f -> p a f", p=128)),
            wait=p.now("vec"), inc=("syn", 16))
        syn_xq_next = p.v("syn")
        for s_i in range(4):
            sub = qt * 4 + s_i
            xqs = xqt[:, (s_i % 2) * DM:(s_i % 2) * DM + DM]
            syn_xq = syn_xq_next
            if s_i < 3:
                p.do("sync", lambda s, sub=sub, s_i=s_i: s.dma_start(
                    out=xqt[:, ((s_i + 1) % 2) * DM:((s_i + 1) % 2) * DM + DM]
                    .unsqueeze(1),
                    in_=xq_d[(sub + 1) * 128:(sub + 2) * 128, :].rearrange(
                        "(a p) f -> p a f", p=128)),
                    wait=p.now("vec"), inc=("syn", 16))
                syn_xq_next = p.v("syn")
            p.do("vector", lambda v, s_i=s_i: v.tensor_scalar(
                f32x[:, :], ao_tm[:, s_i * DM:(s_i + 1) * DM],
                st_mean[:, s_i:s_i + 1], st_rstd[:, s_i:s_i + 1],
                ALU.subtract, ALU.mult))
            p.do("vector", lambda v, xqs=xqs: v.tensor_tensor(
                x1t[:, :], f32x[:, :], xqs, op=ALU.add),
                wait=[("syn", syn_xq), ("gpd", x1_spill)], inc=("vec", 1))
            p.do("gpsimd", lambda g, sub=sub: g.dma_start(
                out=scr_x1[:, sub * DM:(sub + 1) * DM], in_=x1t[:, :]),
                wait=p.now("vec"), inc=("gpd", 16))
            x1_spill = p.v("gpd")
            p.do("vector", lambda v: v.tensor_reduce(
                st_sum[:, 0:1], x1t[:, :], axis=AX.X, op=ALU.add))
            p.do("vector", lambda v: v.tensor_tensor(
                f32x[:, :], x1t[:, :], x1t[:, :], op=ALU.mult))
            p.do("vector", lambda v: v.tensor_reduce(
                st_sq[:, 0:1], f32x[:, :], axis=AX.X, op=ALU.add))
            p.do("vector", lambda v: v.tensor_reduce(
                st_t[:, 1:2], x1t[:, :], axis=AX.X, op=ALU.max))
            p.do("vector", lambda v: v.tensor_reduce(
                st_t[:, 2:3], x1t[:, :], axis=AX.X, op=ALU.min))
            p.do("vector", lambda v: v.tensor_scalar(
                st_mean[:, 0:1], st_sum[:, 0:1], 1.0 / DM, None, ALU.mult),
                inc=("vec", 1))
            p.do("vector", lambda v: v.tensor_tensor(
                st_t[:, 0:1], st_mean[:, 0:1], st_mean[:, 0:1], op=ALU.mult),
                wait=p.now("vec"), inc=("vec", 1))
            p.do("vector", lambda v: v.scalar_tensor_tensor(
                st_var[:, 0:1], st_sq[:, 0:1], 1.0 / DM, st_t[:, 0:1],
                op0=ALU.mult, op1=ALU.subtract), wait=p.now("vec"),
                inc=("vec", 1))
            p.do("scalar", lambda s: s.activation(
                st_rstd[:, 0:1], st_var[:, 0:1], AF.Sqrt, bias=epsc[:, :],
                scale=1.0), wait=p.now("vec"), inc=("sca", 1))
            p.do("vector", lambda v: v.reciprocal(st_rstd[:, 0:1], st_rstd[:, 0:1]),
                 wait=p.now("sca"), inc=("vec", 1))
            p.do("vector", lambda v: v.tensor_tensor(
                st_t[:, 1:2], st_t[:, 1:2], st_mean[:, 0:1], op=ALU.subtract),
                inc=("vec", 1))
            p.do("vector", lambda v: v.tensor_tensor(
                st_t[:, 2:3], st_mean[:, 0:1], st_t[:, 2:3], op=ALU.subtract),
                inc=("vec", 1))
            p.do("vector", lambda v: v.tensor_tensor(
                st_t[:, 1:2], st_t[:, 1:2], st_t[:, 2:3], op=ALU.max),
                wait=p.now("vec"), inc=("vec", 1))
            p.do("vector", lambda v: v.tensor_tensor(
                st_t[:, 1:2], st_t[:, 1:2], st_rstd[:, 0:1], op=ALU.mult),
                wait=p.now("vec"), inc=("vec", 1))
            p.do("vector", lambda v: v.tensor_scalar(
                st_t[:, 1:2], st_t[:, 1:2], EPS, None, ALU.max),
                wait=p.now("vec"), inc=("vec", 1))
            p.do("vector", lambda v, sub=sub: v.tensor_scalar(
                c1sc[:, sub:sub + 1], st_t[:, 1:2], 1.0 / 127.0, None, ALU.mult),
                wait=p.now("vec"), inc=("vec", 1))
            p.do("vector", lambda v: v.reciprocal(st_t[:, 2:3], st_t[:, 1:2]),
                 inc=("vec", 1))
            p.do("vector", lambda v: v.tensor_scalar(
                st_t[:, 2:3], st_t[:, 2:3], 127.0, None, ALU.mult),
                wait=p.now("vec"), inc=("vec", 1))
            p.do("vector", lambda v: v.tensor_tensor(
                st_a[:, 0:1], st_rstd[:, 0:1], st_t[:, 2:3], op=ALU.mult),
                wait=p.now("vec"), inc=("vec", 1))
            p.do("vector", lambda v: v.scalar_tensor_tensor(
                st_b[:, 0:1], st_mean[:, 0:1], -1.0, st_a[:, 0:1],
                op0=ALU.mult, op1=ALU.mult), wait=p.now("vec"), inc=("vec", 1))
            p.do("vector", lambda v: v.tensor_scalar(
                f32x[:, :], x1t[:, :], st_a[:, 0:1], st_b[:, 0:1],
                ALU.mult, ALU.add), wait=p.now("vec"))
            p.do("vector", lambda v: v.tensor_scalar(
                sqd[:, :], f32x[:, :], MAGIC, MAGIC, ALU.add, ALU.subtract),
                wait=[("ten", h1tr_prev[0])], inc=("vec", 1))
            hq_cnt = p.v("vec")
            for hcb in range(HC):
                p.do("tensor", lambda t, hcb=hcb: t.matmul(
                    psMSC[:, hcb * 128:(hcb + 1) * 128],
                    sqd[:, hcb * 128:(hcb + 1) * 128], ident[:, :],
                    start=True, stop=True),
                    wait=[("vec", hq_cnt)],
                    inc=("ten", 1) if hcb == HC - 1 else None)
            h1tr_prev[0] = p.v("ten")
            p.do("vector", lambda v: v.tensor_copy(h1qst[:, :], psMSC[:, :]),
                 wait=p.now("ten") + p.now("gpd"), inc=("vec", 1))
            for hcb in range(HC):
                p.do("gpsimd", lambda g, sub=sub, hcb=hcb: g.dma_start(
                    out=scr_h1q[:, hcb * OWN + sub * 128:
                                hcb * OWN + (sub + 1) * 128],
                    in_=h1qst[:, hcb * 128:(hcb + 1) * 128]),
                    wait=p.now("vec"), inc=("gpd", 16))
            h1_spill = p.v("gpd")
        run_prepass_steps(4)

    run_prepass_steps(len(pp_list))
    b_done = p.now("gps", "gpd", "ten", "vec", "sca", "syn")

    ar.release(mark_AB)

    if "C" not in phases:
        fin0 = p.v("syn")
        p.do("sync", lambda s: s.nop(),
             wait=[("syn", fin0), ("ten", p.v("ten")), ("vec", p.v("vec")),
                   ("sca", p.v("sca")), ("gpd", p.v("gpd"))])
        return _emit(nc, p, es)
    mark_C = ar.mark()
    sbC = ar.sb
    psC = ar.ps

    wgT = sbC("wgT", [128, IC * DM], BF16)
    wuT = sbC("wuT", [128, IC * DM], BF16)
    wdToc = sbC("wdToc", [128, INNER], BF16)
    gqfm = sbC("gqfm", [128, IC * 512], BF16)
    uscr = sbC("uscr", [128, INNER])
    gqtm = sbC("gqtm", [128, INNER], BF16)
    sgsb = sbC("sgsb", [128, 512])
    h1qg = sbC("h1qg", [128, HC * 128], BF16)
    ffntm = sbC("ffntm", [128, DM], BF16)

    psG = psC("psG", [128, 1024])
    psU = psC("psU", [128, 1024])
    psT = psC("psT", [128, 1024])
    psD = psC("psD", [128, 1024])

    do_sb = gqtm
    x1s = uscr[:, 0:DM]
    f32c = uscr[:, DM:2 * DM]
    outst = uscr[:, 2 * DM:3 * DM]

    p.do("sync", lambda s: s.dma_start(out=wgT[:, :], in_=scr_wgT[:, :]),
         wait=b_done, inc=("syn", 16))
    p.do("sync", lambda s: s.dma_start(out=wuT[:, :], in_=scr_wuT[:, :]),
         inc=("syn", 16))
    wgu_ld = p.v("syn")

    def wT_rhs(w, hcb, n512):
        img = w.rearrange("p (i h f) -> p i h f", i=IC, h=HC)
        return img[:, 4 * n512:4 * n512 + 4, hcb, :]

    for grp in range(NGRP):
        g0 = grp * 4
        gu_evac = [p.v("sca")] * 2
        gu_vec = [p.v("vec")] * 2
        gqtr_prev = [0]
        for ti in range(4):
            sub = g0 + ti
            for hcb in range(HC):
                p.do("sync", lambda s, sub=sub, hcb=hcb: s.dma_start(
                    out=h1qg[:, hcb * 128:(hcb + 1) * 128],
                    in_=scr_h1q[:, hcb * OWN + sub * 128:
                                hcb * OWN + sub * 128 + 128]),
                    wait=[("syn", wgu_ld), ("ten", p.v("ten")),
                          ("vec", p.v("vec"))], inc=("syn", 16))
            h1_ld = p.v("syn")
            p.do("vector", lambda v, sub=sub: v.tensor_tensor(
                st_a[:, 0:1], c1sc[:, sub:sub + 1], wsc[:, 0:1], op=ALU.mult),
                wait=[("syn", h1_ld)] + p.now("sca"))
            p.do("vector", lambda v, sub=sub: v.tensor_tensor(
                st_a[:, 1:2], c1sc[:, sub:sub + 1], wsc[:, 1:2], op=ALU.mult),
                inc=("vec", 1))
            cg_cnt = p.v("vec")
            for n512 in range(HC):
                par = n512 % 2
                pg = psG[:, par * 512:par * 512 + 512]
                pu = psU[:, par * 512:par * 512 + 512]
                for hcb in range(HC):
                    p.do("tensor", lambda t, hcb=hcb, n512=n512, ti=ti, pg=pg:
                         t.matmul(
                             pg, h1qg[:, hcb * 128:(hcb + 1) * 128],
                             wT_rhs(wgT, hcb, n512),
                             start=(hcb == 0), stop=(hcb == HC - 1)),
                         wait=[("syn", h1_ld),
                               ("sca", gu_evac[par] if hcb == 0 else 0)],
                         inc=("ten", 1) if hcb == HC - 1 else None)
                for hcb in range(HC):
                    p.do("tensor", lambda t, hcb=hcb, n512=n512, ti=ti, pu=pu:
                         t.matmul(
                             pu, h1qg[:, hcb * 128:(hcb + 1) * 128],
                             wT_rhs(wuT, hcb, n512),
                             start=(hcb == 0), stop=(hcb == HC - 1)),
                         wait=[("vec", gu_vec[par] if hcb == 0 else 0)],
                         inc=("ten", 1) if hcb == HC - 1 else None)
                gu_cnt = p.v("ten")
                p.do("scalar", lambda s, pg=pg: s.activation(
                    sgsb[:, :], pg, AF.Silu, scale=st_a[:, 0:1]),
                    wait=[("ten", gu_cnt), ("vec", cg_cnt)] + p.now("vec"),
                    inc=("sca", 1))
                p.do("vector", lambda v, pu=pu, n512=n512: v.scalar_tensor_tensor(
                    uscr[:, n512 * 512:(n512 + 1) * 512], pu, st_a[:, 1:2],
                    sgsb[:, :], op0=ALU.mult, op1=ALU.mult),
                    wait=p.now("sca"), inc=("vec", 1))
                gu_evac[par] = p.v("sca")
                gu_vec[par] = p.v("vec")
            g_cnt = p.v("vec")
            p.do("vector", lambda v: v.tensor_reduce(
                st_t[:, 0:1], uscr[:, :], axis=AX.X, op=ALU.max,
                apply_absolute_value=True), wait=[("vec", g_cnt)],
                inc=("vec", 1))
            p.do("vector", lambda v: v.tensor_scalar(
                st_t[:, 0:1], st_t[:, 0:1], EPS, None, ALU.max),
                wait=p.now("vec"), inc=("vec", 1))
            p.do("vector", lambda v, sub=sub: v.scalar_tensor_tensor(
                cdwsc[:, sub:sub + 1], st_t[:, 0:1], 1.0 / 127.0, wsc[:, 2:3],
                op0=ALU.mult, op1=ALU.mult), wait=p.now("vec"), inc=("vec", 1))
            p.do("vector", lambda v: v.reciprocal(st_t[:, 1:2], st_t[:, 0:1]),
                 inc=("vec", 1))
            p.do("vector", lambda v: v.tensor_scalar(
                st_t[:, 1:2], st_t[:, 1:2], 127.0, None, ALU.mult),
                wait=p.now("vec"), inc=("vec", 1))
            p.do("vector", lambda v: v.tensor_scalar(
                uscr[:, :], uscr[:, :], st_t[:, 1:2], MAGIC, ALU.mult, ALU.add),
                wait=p.now("vec"))
            p.do("vector", lambda v: v.tensor_scalar(
                gqtm[:, :], uscr[:, :], MAGIC, None, ALU.subtract),
                wait=[("ten", gqtr_prev[0])], inc=("vec", 1))
            rq_cnt = p.v("vec")
            if "G" in phases and grp == 1 and ti == 0:
                dq2f = dbg_q2.bitcast(F32)
                p.do("sync", lambda s: s.dma_start(
                    out=dq2f[:, 0:4096], in_=uscr[:, :]), wait=p.now("vec"),
                    inc=("syn", 16))
                p.do("sync", lambda s: s.dma_start(
                    out=dq2f[:, 4096:4100], in_=st_t[:, 0:4],
                    allow_slow_non_contiguous=True), inc=("syn", 16))
                p.do("sync", lambda s, sub=sub: s.dma_start(
                    out=dq2f[:, 4100:4101], in_=cdwsc[:, sub:sub + 1],
                    allow_slow_non_contiguous=True), inc=("syn", 16))
                _dg = p.v("syn")
                p.do("vector", lambda v: v.nop(), wait=[("syn", _dg)],
                     inc=("vec", 1))
            gq3 = gqfm[:, :].rearrange("p (i t) -> p i t", i=IC)
            tr_ev = 0
            for fill in range(4):
                for j in range(HC):
                    icb = fill * 8 + j
                    p.do("tensor", lambda t, icb=icb, j=j: t.matmul(
                        psT[:, j * 128:(j + 1) * 128],
                        gqtm[:, icb * 128:(icb + 1) * 128],
                        ident[:, :], start=True, stop=True),
                        wait=[("vec", rq_cnt if fill == 0 else tr_ev)],
                        inc=("ten", 1) if j == HC - 1 else None)
                p.do("vector", lambda v, fill=fill, ti=ti, gq3=gq3: v.tensor_copy(
                    gq3[:, fill * 8:(fill + 1) * 8, ti * 128:(ti + 1) * 128],
                    psT[:, :].rearrange("p (i f) -> p i f", i=HC)),
                    wait=p.now("ten"), inc=("vec", 1))
                tr_ev = p.v("vec")
                gqtr_prev[0] = p.v("ten")
        if "G" in phases and grp == 1:
            p.do("sync", lambda s: s.dma_start(
                out=dbg_q3[:, :], in_=gqfm[:, :]),
                wait=p.now("vec"), inc=("syn", 16))
            _dg2 = p.v("syn")
            p.do("tensor", lambda t: t.nop(), wait=[("syn", _dg2)],
                 inc=("ten", 1))
        gq_ready = p.v("vec")
        do_evac = [p.v("vec")] * 2
        for oc in range(HC):
            p.do("sync", lambda s, oc=oc: s.dma_start(
                out=wdToc[:, :], in_=scr_wdT[:, oc * INNER:(oc + 1) * INNER]),
                wait=p.now("ten"), inc=("syn", 16))
            wd_ld = p.v("syn")
            par = oc % 2
            pd = psD[:, par * 512:par * 512 + 512]
            for icb in range(IC):
                p.do("tensor", lambda t, icb=icb, pd=pd: t.matmul(
                    pd, wdToc[:, icb * 128:(icb + 1) * 128],
                    gqfm[:, icb * 512:(icb + 1) * 512],
                    start=(icb == 0), stop=(icb == IC - 1)),
                    wait=[("syn", wd_ld), ("vec", gq_ready),
                          ("vec", do_evac[par] if icb == 0 else 0)],
                    inc=("ten", 1) if icb == IC - 1 else None)
            p.do("vector", lambda v, oc=oc, pd=pd: v.tensor_copy(
                do_sb[:, oc * 512:(oc + 1) * 512], pd),
                wait=p.now("ten"), inc=("vec", 1))
            do_evac[par] = p.v("vec")
        ftr_ev = p.v("vec")
        out_prev = [0]
        for ti in range(4):
            sub = g0 + ti
            for oc in range(HC):
                p.do("tensor", lambda t, oc=oc, ti=ti: t.matmul(
                    psT[:, oc * 128:(oc + 1) * 128],
                    do_sb[:, oc * 512 + ti * 128: oc * 512 + (ti + 1) * 128],
                    ident[:, :], start=True, stop=True),
                    wait=[("vec", ftr_ev)],
                    inc=("ten", 1) if oc == HC - 1 else None)
            p.do("vector", lambda v: v.tensor_copy(ffntm[:, :], psT[:, :]),
                 wait=p.now("ten"), inc=("vec", 1))
            ftr_ev = p.v("vec")
            p.do("sync", lambda s, sub=sub: s.dma_start(
                out=x1s, in_=scr_x1[:, sub * DM:(sub + 1) * DM]),
                wait=[("vec", ftr_ev)], inc=("syn", 16))
            syn_x1 = p.v("syn")
            p.do("vector", lambda v: v.tensor_reduce(
                st_sum[:, 0:1], ffntm[:, :], axis=AX.X, op=ALU.add))
            p.do("vector", lambda v: v.tensor_tensor(
                f32c, ffntm[:, :], ffntm[:, :], op=ALU.mult))
            p.do("vector", lambda v: v.tensor_reduce(
                st_sq[:, 0:1], f32c, axis=AX.X, op=ALU.add))
            p.do("vector", lambda v, sub=sub: v.scalar_tensor_tensor(
                st_mean[:, 0:1], st_sum[:, 0:1], 1.0 / DM, cdwsc[:, sub:sub + 1],
                op0=ALU.mult, op1=ALU.mult), inc=("vec", 1))
            p.do("vector", lambda v, sub=sub: v.tensor_tensor(
                st_t[:, 0:1], cdwsc[:, sub:sub + 1], cdwsc[:, sub:sub + 1],
                op=ALU.mult), inc=("vec", 1))
            p.do("vector", lambda v: v.tensor_tensor(
                st_t[:, 0:1], st_t[:, 0:1], st_sq[:, 0:1], op=ALU.mult),
                wait=p.now("vec"), inc=("vec", 1))
            p.do("vector", lambda v: v.tensor_tensor(
                st_t2[:, 0:1], st_mean[:, 0:1], st_mean[:, 0:1], op=ALU.mult),
                wait=p.now("vec"), inc=("vec", 1))
            p.do("vector", lambda v: v.scalar_tensor_tensor(
                st_var[:, 0:1], st_t[:, 0:1], 1.0 / DM, st_t2[:, 0:1],
                op0=ALU.mult, op1=ALU.subtract), wait=p.now("vec"),
                inc=("vec", 1))
            p.do("scalar", lambda s: s.activation(
                st_rstd[:, 0:1], st_var[:, 0:1], AF.Sqrt, bias=epsc[:, :],
                scale=1.0), wait=p.now("vec"), inc=("sca", 1))
            p.do("vector", lambda v: v.reciprocal(st_rstd[:, 0:1], st_rstd[:, 0:1]),
                 wait=p.now("sca"), inc=("vec", 1))
            p.do("vector", lambda v, sub=sub: v.tensor_tensor(
                st_a[:, 0:1], cdwsc[:, sub:sub + 1], st_rstd[:, 0:1], op=ALU.mult),
                wait=p.now("vec"), inc=("vec", 1))
            p.do("vector", lambda v: v.scalar_tensor_tensor(
                st_b[:, 0:1], st_mean[:, 0:1], -1.0, st_rstd[:, 0:1],
                op0=ALU.mult, op1=ALU.mult), wait=p.now("vec"), inc=("vec", 1))
            p.do("vector", lambda v: v.tensor_scalar(
                f32c, ffntm[:, :], st_a[:, 0:1], st_b[:, 0:1],
                ALU.mult, ALU.add), wait=[("syn", syn_x1)] + p.now("vec"))
            p.do("vector", lambda v: v.tensor_tensor(
                outst, f32c, x1s, op=ALU.add), inc=("vec", 1))
            p.do("sync", lambda s, sub=sub: s.dma_start(
                out=out_d[sub * 128:(sub + 1) * 128, :].rearrange(
                    "(a p) f -> p a f", p=128),
                in_=outst.unsqueeze(1)),
                wait=p.now("vec"), inc=("syn", 16))
            out_prev[0] = p.v("syn")

    for i in range(6):
        p.do("sync", lambda s: s.dma_start(
            out=gqtm[:, 0:128].bitcast(F32)[:, 0:64], in_=out_d[0:128, 0:64]),
            wait=p.now("syn"), inc=("syn", 16))
    fin = p.v("syn")
    p.do("sync", lambda s: s.nop(), wait=[("syn", fin)])

    ar.release(mark_C)
    return _emit(nc, p, es)


def _emit(nc, p, es):
    with nc.Block() as block:
        @block.sync
        def _(e):
            for fn in p.ops["sync"]:
                fn(e)

        @block.gpsimd
        def _(e):
            for fn in p.ops["gpsimd"]:
                fn(e)

        @block.scalar
        def _(e):
            for fn in p.ops["scalar"]:
                fn(e)

        @block.vector
        def _(e):
            for fn in p.ops["vector"]:
                fn(e)

        @block.tensor
        def _(e):
            for fn in p.ops["tensor"]:
                fn(e)

    es.close()
    return nc


_NC_CACHE = {}


def _get_nc(seq):
    if seq not in _NC_CACHE:
        _NC_CACHE[seq] = build_nc(seq)
    return _NC_CACHE[seq]


def kernel(L0_attn_X, L0_attn_W_qkv, L0_attn_W_out, L0_ffn_w_gate,
           L0_ffn_w_up, L0_ffn_w_down):
    X = np.ascontiguousarray(np.asarray(L0_attn_X, dtype=np.float32))
    Wqkv = np.asarray(L0_attn_W_qkv, dtype=np.float32)
    B, S, _ = X.shape
    OWN = S // 2
    idx = np.arange(3 * DM).reshape(NH, DH, 3)
    wq = np.ascontiguousarray(Wqkv[:, idx[:, :, 0].reshape(-1)])
    wk = np.ascontiguousarray(Wqkv[:, idx[:, :, 1].reshape(-1)])
    wv = np.ascontiguousarray(Wqkv[:, idx[:, :, 2].reshape(-1)])
    wo = np.ascontiguousarray(np.asarray(L0_attn_W_out, dtype=np.float32))
    wg = np.ascontiguousarray(np.asarray(L0_ffn_w_gate, dtype=np.float32))
    wu = np.ascontiguousarray(np.asarray(L0_ffn_w_up, dtype=np.float32))
    wd = np.ascontiguousarray(np.asarray(L0_ffn_w_down, dtype=np.float32))

    nc = _get_nc(S)
    in_maps = []
    ncore = 2 * B
    for c in range(ncore):
        b, h = c // 2, c % 2
        in_maps.append(dict(
            xb=X[b], xq=np.ascontiguousarray(X[b, h * OWN:(h + 1) * OWN]),
            wq=wq, wk=wk, wv=wv, wo=wo, wg=wg, wu=wu, wd=wd))
    res = run_bass_kernel_spmd(nc, in_maps, core_ids=list(range(ncore)))
    out = np.empty((B, S, DM), np.float32)
    for c in range(ncore):
        b, h = c // 2, c % 2
        out[b, h * OWN:(h + 1) * OWN] = res.results[c]["out"]
    return out


# version 15 (final)
# speedup vs baseline: 1.1834x; 1.1834x over previous
import numpy as np
from contextlib import ExitStack

import concourse.bass as bass
import concourse.mybir as mybir
from concourse.bass_utils import run_bass_kernel_spmd

F32 = mybir.dt.float32
BF16 = mybir.dt.bfloat16
AF = mybir.ActivationFunctionType
ALU = mybir.AluOpType
AX = mybir.AxisListType

MAGIC = 12582912.0
EPS = 1e-5
DM = 1024
HC = DM // 128
NH = 16
DH = 64
INNER = 4096
IC = INNER // 128
PST = 160
VST = HC * PST

ENGINES = ("sync", "gpsimd", "scalar", "vector", "tensor")


class Prog:

    def __init__(self, nc, es):
        self.nc = nc
        self.ops = {e: [] for e in ENGINES}
        self.cnt = {}
        self.h = {}
        self.lastw = {e: {} for e in ENGINES}
        for name in ("syn", "gps", "gpd", "ten", "vec", "sca"):
            self.h[name] = es.enter_context(nc.semaphore(name))
            self.cnt[name] = 0

    def v(self, name):
        return self.cnt[name]

    DMA_DEPTH = 3

    def do(self, eng, fn, wait=(), inc=None):
        if eng == "sync" and getattr(self, "route_sync_to_gpsimd", False):
            eng = "gpsimd"
        wait = list(wait)
        if inc is not None and inc[0] in ("syn", "gpd"):
            wait.append((inc[0], self.cnt[inc[0]] - 16 * (self.DMA_DEPTH - 1)))
        waits = []
        for (n, val) in wait:
            if val <= 0 or self.lastw[eng].get(n, -1) >= val:
                continue
            self.lastw[eng][n] = val
            waits.append((self.h[n], val))
        if inc is not None:
            iname, iamt = inc
            self.cnt[iname] += iamt
            ih = self.h[iname]
        else:
            ih, iamt = None, 0

        def run(e, waits=waits, fn=fn, ih=ih, iamt=iamt):
            for hh, val in waits:
                e.wait_ge(hh, val)
            ins = fn(e)
            if ih is not None:
                ins.then_inc(ih, iamt)

        self.ops[eng].append(run)

    def now(self, *names):
        return [(n, self.cnt[n]) for n in names]


class Arena:

    def __init__(self, nc, es, words, psum_banks=8):
        h = es.enter_context(nc.sbuf_tensor("arena", [128, words], F32))
        self.f = h[:, :]
        self.b = self.f.bitcast(BF16)
        self.words = words
        self.top = 0
        ph = es.enter_context(nc.psum_tensor("psarena", [128, psum_banks * 512], F32))
        self.pf = ph[:, :]
        self.ptop = 0
        self.pbanks = psum_banks

    def sb(self, name, shape, dt=F32):
        assert len(shape) == 2 and shape[0] == 128
        n = shape[1]
        words = n if dt == F32 else (n + 1) // 2
        off = (self.top + 15) & ~15
        self.top = off + words
        assert self.top <= self.words, f"SBUF arena overflow at {name}"
        if dt == F32:
            return self.f[:, off:off + n]
        return self.b[:, 2 * off:2 * off + n]

    def ps(self, name, shape):
        assert shape[0] == 128 and shape[1] % 512 == 0
        nb = shape[1] // 512
        off = self.ptop
        self.ptop = off + nb
        assert self.ptop <= self.pbanks, f"PSUM arena overflow at {name}"
        return self.pf[:, off * 512:(off + nb) * 512]

    def mark(self):
        return (self.top, self.ptop)

    def release(self, m):
        self.top, self.ptop = m


def build_nc(SEQ=4096, phases="ABC", dbg_out=False):
    assert SEQ % 1024 == 0
    OWN = SEQ // 2
    KC = SEQ // 128
    NTA = SEQ // 512
    NTQ = OWN // 512
    NSUB = OWN // 128
    KP = KC // 2
    NGRP = OWN // 512

    nc = bass.Bass("TRN2", target_bir_lowering=False, detect_race_conditions=False)
    nc.reset()

    xb_d = nc.dram_tensor("xb", [SEQ, DM], F32, kind="ExternalInput").ap()
    xq_d = nc.dram_tensor("xq", [OWN, DM], F32, kind="ExternalInput").ap()
    wq_d = nc.dram_tensor("wq", [DM, DM], F32, kind="ExternalInput").ap()
    wk_d = nc.dram_tensor("wk", [DM, DM], F32, kind="ExternalInput").ap()
    wv_d = nc.dram_tensor("wv", [DM, DM], F32, kind="ExternalInput").ap()
    wo_d = nc.dram_tensor("wo", [DM, DM], F32, kind="ExternalInput").ap()
    wg_d = nc.dram_tensor("wg", [INNER, DM], F32, kind="ExternalInput").ap()
    wu_d = nc.dram_tensor("wu", [INNER, DM], F32, kind="ExternalInput").ap()
    wd_d = nc.dram_tensor("wd", [DM, INNER], F32, kind="ExternalInput").ap()
    out_d = nc.dram_tensor("out", [OWN, DM], F32, kind="ExternalOutput").ap()

    K_I = "ExternalOutput"
    scr_K = nc.dram_tensor("scr_K", [128, HC * SEQ], BF16, kind=K_I).ap()
    dbg_q = nc.dram_tensor("dbg_q", [128, HC * OWN], BF16, kind=K_I).ap()
    dbg_k = nc.dram_tensor("dbg_k", [128, SEQ], BF16, kind=K_I).ap()
    dbg_e = nc.dram_tensor("dbg_e", [128, 4096], BF16, kind=K_I).ap()
    dbg_q2 = nc.dram_tensor("dbg_q2", [128, HC * OWN], BF16, kind=K_I).ap()
    dbg_q3 = nc.dram_tensor("dbg_q3", [128, HC * OWN], BF16, kind=K_I).ap()
    scr_x1 = nc.dram_tensor("scr_x1", [128, NSUB * DM], F32, kind=K_I).ap()
    scr_h1q = nc.dram_tensor("scr_h1q", [128, HC * OWN], BF16, kind=K_I).ap()
    scr_wgT = nc.dram_tensor("scr_wgT", [128, IC * DM], BF16, kind=K_I).ap()
    scr_wuT = nc.dram_tensor("scr_wuT", [128, IC * DM], BF16, kind=K_I).ap()
    scr_wdT = nc.dram_tensor("scr_wdT", [128, HC * INNER], BF16, kind=K_I).ap()

    es = ExitStack()
    ar = Arena(nc, es, 53200)
    sb = ar.sb
    p = Prog(nc, es)
    p.route_sync_to_gpsimd = False

    ident = sb("ident", [128, 128], BF16)
    ones128 = sb("ones128", [128, 128])
    epsc = sb("epsc", [128, 1])
    zpad = sb("zpad", [128, 1])
    onecol = sb("onecol", [128, 1])
    c1sc = sb("c1sc", [128, NSUB])
    cdwsc = sb("cdwsc", [128, NSUB])
    wsc = sb("wsc", [128, 4])
    swb = sb("swb", [128, 1])
    st_sum = sb("st_sum", [128, 4])
    st_sq = sb("st_sq", [128, 4])
    st_mean = sb("st_mean", [128, 4])
    st_var = sb("st_var", [128, 4])
    st_rstd = sb("st_rstd", [128, 4])
    st_t = sb("st_t", [128, 4])
    st_t2 = sb("st_t2", [128, 4])
    st_a = sb("st_a", [128, 4])
    st_b = sb("st_b", [128, 4])
    sqd = sb("sqd", [128, DM], BF16)

    p.do("gpsimd", lambda g: g.memset(ident[:, :], 0.0), inc=("gps", 1))
    p.do("gpsimd", lambda g: g.affine_select(
        out=ident[:, :], in_=ident[:, :], compare_op=ALU.not_equal,
        fill=1.0, base=0, pattern=[[-1, 128]], channel_multiplier=1),
        wait=p.now("gps"), inc=("gps", 1))
    p.do("gpsimd", lambda g: g.memset(ones128[:, :], 1.0), inc=("gps", 1))
    p.do("gpsimd", lambda g: g.memset(epsc[:, :], EPS), inc=("gps", 1))
    p.do("gpsimd", lambda g: g.memset(zpad[:, :], 0.0), inc=("gps", 1))
    p.do("gpsimd", lambda g: g.memset(onecol[:, :], 1.0), inc=("gps", 1))

    settle = sb("settle", [128, 16])
    for i in range(12):
        p.do("sync", lambda s, i=i: s.dma_start(
            out=settle[:, :], in_=xb_d[0:128, i * 16:(i + 1) * 16]),
            wait=p.now("syn"), inc=("syn", 16))
    for i in range(4):
        p.do("gpsimd", lambda g, i=i: g.dma_start(
            out=settle[:, :], in_=xq_d[0:128, i * 16:(i + 1) * 16]),
            wait=p.now("gpd"), inc=("gpd", 16))
    _settled = [("syn", p.v("syn")), ("gpd", p.v("gpd"))]
    p.do("vector", lambda v: v.nop(), wait=_settled, inc=("vec", 1))
    p.do("scalar", lambda s: s.nop(), wait=_settled, inc=("sca", 1))
    p.do("tensor", lambda t: t.nop(), wait=_settled, inc=("ten", 1))

    p.do("vector", lambda v: v.memset(settle[:, 0:1], 1.0), inc=("vec", 1))
    _pace0 = p.v("vec")

    def emit_pace(n=1):
        for _ in range(n):
            def tick(v):
                for _ in range(60):
                    v.tensor_scalar(settle[:, 0:1], settle[:, 0:1], 1.0, None,
                                    ALU.mult)
                return v.nop()
            p.do("vector", tick, inc=("vec", 1))

    mark_AB = ar.mark()
    sbAB = ar.sb
    vaug = sbAB("vaug", [128, KC * VST], BF16)
    qfm = sbAB("qfm", [128, HC * OWN], BF16)

    p.do("gpsimd", lambda g: g.memset(vaug[:, :], 0.0), inc=("gps", 1))
    p.do("gpsimd", lambda g: g.memset(
        vaug[:, :].rearrange("p (k j c) -> p k j c", k=KC, j=HC)[:, :, :, 64:65],
        1.0), wait=p.now("gps"), inc=("gps", 1))
    gps_const = p.v("gps")

    def emit_ln_stats(xin, nsub, xwait):
        for s_i in range(nsub):
            xs = xin[:, s_i * DM:(s_i + 1) * DM]
            p.do("vector", lambda v, xs=xs, s_i=s_i: v.tensor_reduce(
                st_sum[:, s_i:s_i + 1], xs, axis=AX.X, op=ALU.add),
                wait=xwait, inc=("vec", 1))
            p.do("scalar", lambda s, xs=xs, s_i=s_i: s.activation(
                sqd[:, :], xs, AF.Square, accum_out=st_sq[:, s_i:s_i + 1]),
                wait=list(xwait) + [("sca", p.v("sca"))], inc=("sca", 1))
        p.do("vector", lambda v, nsub=nsub: v.tensor_scalar(
            st_mean[:, 0:nsub], st_sum[:, 0:nsub], 1.0 / DM, None, ALU.mult),
            wait=p.now("sca"), inc=("vec", 1))
        p.do("vector", lambda v, nsub=nsub: v.tensor_tensor(
            st_t[:, 0:nsub], st_mean[:, 0:nsub], st_mean[:, 0:nsub], op=ALU.mult),
            wait=p.now("vec"), inc=("vec", 1))
        p.do("vector", lambda v, nsub=nsub: v.scalar_tensor_tensor(
            st_var[:, 0:nsub], st_sq[:, 0:nsub], 1.0 / DM, st_t[:, 0:nsub],
            op0=ALU.mult, op1=ALU.subtract), wait=p.now("vec"), inc=("vec", 1))
        p.do("scalar", lambda s, nsub=nsub: s.activation(
            st_rstd[:, 0:nsub], st_var[:, 0:nsub], AF.Sqrt, bias=epsc[:, :],
            scale=1.0), wait=p.now("vec") + [("gps", gps_const)], inc=("sca", 1))
        p.do("vector", lambda v, nsub=nsub: v.reciprocal(
            st_rstd[:, 0:nsub], st_rstd[:, 0:nsub]),
            wait=p.now("sca"), inc=("vec", 1))

    mark_A = ar.mark()
    sbA = ar.sb
    w1_sb = sbA("w1_sb", [128, HC * DM], BF16)
    w2_sb = sbA("w2_sb", [128, HC * DM], BF16)
    xta = sbA("xta", [128, 4 * DM])
    lntm = sbA("lntm", [128, 4 * DM], BF16)
    lnt = sbA("lnt", [128, 2 * 4096], BF16)
    ksp = sbA("ksp", [128, HC * 512], BF16)
    nbias = sbA("nbias", [128, 4])
    pA_tr = ar.ps("pA_tr", [128, 1024])
    pA_o = ar.ps("pA_o", [128, 1024])
    pA_v = ar.ps("pA_v", [128, 1024])

    def wload(dst, src, wait=(), paced=False):
        for c in range(HC):
            if paced:
                emit_pace(1)
                wait = list(wait) + p.now("vec")
            p.do("gpsimd", lambda g, dst=dst, src=src, c=c: g.dma_start(
                out=dst[:, c * DM:(c + 1) * DM],
                in_=src[c * 128:(c + 1) * 128, :]),
                wait=wait, inc=("gpd", 16))

    wload(w1_sb, wk_d, paced=True)
    wload(w2_sb, wv_d, paced=True)
    wkv_ld = p.v("gpd")
    emit_pace(8)


    lnt_consumed = [0, 0]
    stateA = {"norm": 0, "tr_evac": 0, "o_evac": [0, 0], "v_evac": 0,
              "k_spill": 0, "lntm_rd": 0}

    def phaseA_tile(t_i, x_src, row0, do_kv, do_q, qcol0, w_ld):
        buf = t_i % 2
        lt3 = lnt[:, buf * 4096:(buf + 1) * 4096].rearrange(
            "p (h f) -> p h f", h=HC)
        for a in range(4):
            p.do("sync", lambda s, a=a: s.dma_start(
                out=xta[:, a * DM:(a + 1) * DM],
                in_=x_src[row0 + a * 128:row0 + (a + 1) * 128, :]),
                wait=[("sca", stateA["norm"]), ("vec", p.v("vec"))],
                inc=("syn", 16))
        xw = [("syn", p.v("syn")), ("vec", p.v("vec"))]
        emit_ln_stats(xta, 4, xw)
        p.do("vector", lambda v: v.scalar_tensor_tensor(
            nbias[:, 0:4], st_mean[:, 0:4], -1.0, st_rstd[:, 0:4],
            op0=ALU.mult, op1=ALU.mult), wait=p.now("vec"), inc=("vec", 1))
        for s_i in range(4):
            p.do("scalar", lambda s, s_i=s_i: s.activation(
                lntm[:, s_i * DM:(s_i + 1) * DM], xta[:, s_i * DM:(s_i + 1) * DM],
                AF.Identity, bias=nbias[:, s_i:s_i + 1],
                scale=st_rstd[:, s_i:s_i + 1]),
                wait=p.now("vec") + [("ten", stateA["lntm_rd"])], inc=("sca", 1))
        stateA["norm"] = p.v("sca")
        if "E" in phases and t_i == 0:
            p.do("sync", lambda s: s.dma_start(out=dbg_e[:, :], in_=lntm[:, :]),
                 wait=p.now("sca"), inc=("syn", 16))
            p.do("sync", lambda s: s.dma_start(
                out=dbg_k[:, 0:8].bitcast(F32),
                in_=st_rstd[:, 0:4], allow_slow_non_contiguous=True), inc=("syn", 16))
            p.do("sync", lambda s: s.dma_start(
                out=dbg_k[:, 16:24].bitcast(F32),
                in_=st_mean[:, 0:4], allow_slow_non_contiguous=True), inc=("syn", 16))
            _dcnt = p.v("syn")
            p.do("vector", lambda v: v.nop(), wait=[("syn", _dcnt)], inc=("vec", 1))
            p.do("scalar", lambda s: s.nop(), wait=[("syn", _dcnt)], inc=("sca", 1))
            p.do("tensor", lambda t: t.nop(), wait=[("syn", _dcnt)], inc=("ten", 1))
        for s_i in range(4):
            for hcb in range(HC):
                p.do("tensor", lambda t, s_i=s_i, hcb=hcb: t.matmul(
                    pA_tr[:, hcb * 128:(hcb + 1) * 128],
                    lntm[:, s_i * DM + hcb * 128: s_i * DM + (hcb + 1) * 128],
                    ident[:, :], start=True, stop=True),
                    wait=[("sca", stateA["norm"]),
                          ("vec", stateA["tr_evac"]),
                          ("gps", gps_const),
                          ("ten", lnt_consumed[buf] if (s_i == 0 and hcb == 0) else 0)],
                    inc=("ten", 1) if hcb == HC - 1 else None)
            p.do("vector", lambda v, s_i=s_i, lt3=lt3: v.tensor_copy(
                lt3[:, :, s_i * 128:(s_i + 1) * 128],
                pA_tr[:, :].rearrange("p (h f) -> p h f", h=HC)),
                wait=p.now("ten"), inc=("vec", 1))
            stateA["tr_evac"] = p.v("vec")
        stateA["lntm_rd"] = p.v("ten")
        lnt_ready = p.v("vec")
        if "E" in phases and t_i == 0 and do_kv:
            p.do("sync", lambda s: s.dma_start(out=dbg_q2[:, 0:HC*DM],
                 in_=w1_sb[:, :]), wait=p.now("vec"), inc=("syn", 16))
            p.do("sync", lambda s, lt3=lt3: s.dma_start(
                 out=dbg_q3[:, 0:4096],
                 in_=lnt[:, (t_i % 2) * 4096:(t_i % 2) * 4096 + 4096]),
                 inc=("syn", 16))
            _d2 = p.v("syn")
            p.do("vector", lambda v: v.nop(), wait=[("syn", _d2)], inc=("vec", 1))
            p.do("scalar", lambda s: s.nop(), wait=[("syn", _d2)], inc=("sca", 1))
            p.do("tensor", lambda t: t.nop(), wait=[("syn", _d2)], inc=("ten", 1))
        if do_kv:
            for fc in range(HC):
                par = fc % 2
                pk = pA_o[:, par * 512:par * 512 + 512]
                for hcb in range(HC):
                    p.do("tensor", lambda t, fc=fc, hcb=hcb, pk=pk, lt3=lt3: t.matmul(
                        pk, w1_sb[:, hcb * DM + fc * 128: hcb * DM + (fc + 1) * 128],
                        lt3[:, hcb, :], start=(hcb == 0), stop=(hcb == HC - 1)),
                        wait=[("vec", lnt_ready), ("gpd", w_ld),
                              ("vec", stateA["o_evac"][par] if hcb == 0 else 0)],
                        inc=("ten", 1) if hcb == HC - 1 else None)
                p.do("vector", lambda v, fc=fc, pk=pk: v.tensor_copy(
                    ksp[:, fc * 512:(fc + 1) * 512], pk),
                    wait=p.now("ten") + [("syn", stateA["k_spill"])],
                    inc=("vec", 1))
                stateA["o_evac"][par] = p.v("vec")
            for fc in range(HC):
                p.do("sync", lambda s, row0=row0, fc=fc: s.dma_start(
                    out=scr_K[:, fc * SEQ + row0: fc * SEQ + row0 + 512],
                    in_=ksp[:, fc * 512:(fc + 1) * 512]),
                    wait=p.now("vec"), inc=("syn", 16))
            stateA["k_spill"] = p.v("syn")
            for s_i in range(4):
                kc = row0 // 128 + s_i
                for hcb in range(HC):
                    for nh in range(2):
                        pv = pA_v[:, nh * 512:nh * 512 + 512]
                        p.do("tensor", lambda t, s_i=s_i, hcb=hcb, nh=nh, pv=pv,
                             lt3=lt3: t.matmul(
                            pv, lt3[:, hcb, s_i * 128:(s_i + 1) * 128],
                            w2_sb[:, hcb * DM + nh * 512: hcb * DM + (nh + 1) * 512],
                            start=(hcb == 0), stop=(hcb == HC - 1)),
                            wait=[("vec", lnt_ready),
                                  ("vec", stateA["v_evac"] if (hcb == 0 and nh == 0) else 0)],
                            inc=("ten", 1) if (hcb == HC - 1 and nh == 1) else None)
                p.do("vector", lambda v, kc=kc: v.tensor_copy(
                    vaug[:, kc * VST: kc * VST + VST].rearrange(
                        "p (j c) -> p j c", j=HC)[:, :, 0:64],
                    pA_v[:, :].rearrange("p (j c) -> p j c", j=HC)[:, :, 0:64]),
                    wait=p.now("ten") + [("gps", gps_const)], inc=("vec", 1))
                p.do("vector", lambda v, kc=kc: v.tensor_copy(
                    vaug[:, kc * VST: kc * VST + VST].rearrange(
                        "p (j c) -> p j c", j=HC)[:, :, 96:160],
                    pA_v[:, :].rearrange("p (j c) -> p j c", j=HC)[:, :, 64:128]),
                    inc=("vec", 1))
                stateA["v_evac"] = p.v("vec")
        if do_q:
            for fc in range(HC):
                par = fc % 2
                pk = pA_o[:, par * 512:par * 512 + 512]
                for hcb in range(HC):
                    p.do("tensor", lambda t, fc=fc, hcb=hcb, pk=pk, lt3=lt3: t.matmul(
                        pk, w1_sb[:, hcb * DM + fc * 128: hcb * DM + (fc + 1) * 128],
                        lt3[:, hcb, :], start=(hcb == 0), stop=(hcb == HC - 1)),
                        wait=[("vec", lnt_ready), ("gpd", w_ld),
                              ("vec", stateA["o_evac"][par] if hcb == 0 else 0)],
                        inc=("ten", 1) if hcb == HC - 1 else None)
                p.do("vector", lambda v, fc=fc, pk=pk, qcol0=qcol0: v.tensor_copy(
                    qfm[:, fc * OWN + qcol0: fc * OWN + qcol0 + 512], pk),
                    wait=p.now("ten"), inc=("vec", 1))
                stateA["o_evac"][par] = p.v("vec")
        lnt_consumed[buf] = p.v("ten")

    for t_i in range(NTA):
        phaseA_tile(t_i, xb_d, t_i * 512, True, False, 0, wkv_ld)
    wload(w1_sb, wq_d, wait=p.now("ten"))
    wq_ld = p.v("gpd")
    for t_i in range(NTQ):
        phaseA_tile(NTA + t_i, xq_d, t_i * 512, False, True, t_i * 512, wq_ld)

    ar.release(mark_A)

    if "B" not in phases:
        if "D" in phases:
            p.do("sync", lambda s: s.dma_start(out=dbg_q[:, :], in_=qfm[:, :]),
                 wait=p.now("vec"), inc=("syn", 16))
            for a in range(4):
                p.do("sync", lambda s, a=a: s.dma_start(
                    out=xta[:, a * DM:(a + 1) * DM],
                    in_=xb_d[a * 128:(a + 1) * 128, :]),
                    wait=p.now("vec", "ten", "sca"), inc=("syn", 16))
            p.do("sync", lambda s: s.dma_start(out=dbg_e[:, 0:4096].bitcast(F32)[:, 0:2048],
                 in_=xta[:, 0:2048]), inc=("syn", 16))
            p.do("sync", lambda s: s.dma_start(out=dbg_k[:, 0:2048].bitcast(F32)[:, 0:1024],
                 in_=xta[:, 2048:3072]), inc=("syn", 16))
        fin0 = p.v("syn")
        p.do("sync", lambda s: s.nop(),
             wait=[("syn", fin0), ("ten", p.v("ten")), ("vec", p.v("vec")),
                   ("sca", p.v("sca")), ("gpd", p.v("gpd"))])
        return _emit(nc, p, es)
    mark_B = ar.mark()
    sbB = ar.sb
    psB = ar.ps

    ksl = sbB("ksl", [128, 2 * SEQ], BF16)
    wo_sb = sbB("wo_sb", [128, HC * DM], BF16)
    expt = sbB("expt", [128, 2 * 2048], BF16)
    zrow = sbB("zrow", [128, 1024])
    attn = sbB("attn", [128, HC * 512], BF16)
    ao_sb = sbB("ao_sb", [128, HC * 512], BF16)
    ao_tm = sbB("ao_tm", [128, 4 * DM], BF16)
    xqt = sbB("xqt", [128, 2 * DM])
    x1t = sbB("x1t", [128, DM])
    f32x = sbB("f32x", [128, DM])
    h1qst = sbB("h1qst", [128, DM], BF16)
    wraw = sbB("wraw", [128, 1024])
    wpos = sqd[:, 0:512]
    wneg = h1qst[:, 512:1024]
    wqn = sbB("wqn", [128, 512], BF16)
    wqt = h1qst[:, 0:512]
    r8 = sbB("r8", [128, 66])

    psS_A = psB("psS_A", [128, 1024])
    psS_B = psB("psS_B", [128, 1024])
    psAV_A = psB("psAV_A", [128, 512])
    psAV_B = psB("psAV_B", [128, 512])
    psMSC = psB("psMSC", [128, 1024])
    psPP = psMSC[:, 0:512]

    a_end = p.now("ten", "vec", "sca")
    p.do("gpsimd", lambda g: g.memset(zrow[:, :], 0.0), wait=a_end, inc=("gps", 1))
    wload(wo_sb, wo_d, wait=a_end)
    wo_ld = p.v("gpd")
    if "D" in phases:
        p.do("sync", lambda s: s.dma_start(out=dbg_q2[:, :], in_=qfm[:, :]),
             wait=p.now("vec"), inc=("syn", 16))
    p.do("tensor", lambda t: t.nop(), wait=p.now("vec", "gps"), inc=("ten", 1))
    p.do("scalar", lambda s: s.nop(), wait=p.now("vec", "gps"), inc=("sca", 1))

    def make_prepass():
        mats = [(wg_d, scr_wgT, 0, False), (wu_d, scr_wuT, 1, False),
                (wd_d, scr_wdT, 2, True)]
        for (src, dst, col, iswd) in mats:
            nj = 8 if iswd else 2
            src4 = src.rearrange("(c p) (j f) -> c j p f", p=128, f=512)
            src3 = [src4[i // nj, i % nj] for i in range(64)]
            for piece in range(65):
                def absstep(piece=piece, src3=src3):
                    if piece < 64:
                        buf = wraw[:, (piece % 2) * 512:(piece % 2) * 512 + 512]
                        p.do("sync", lambda s, buf=buf, piece=piece: s.dma_start(
                            out=buf, in_=src3[piece]),
                            wait=p.now("vec"), inc=("syn", 16))
                    if piece > 0:
                        pv = piece - 1
                        buf = wraw[:, (pv % 2) * 512:(pv % 2) * 512 + 512]
                        p.do("vector", lambda v, pv=pv, buf=buf: v.tensor_reduce(
                            r8[:, pv:pv + 1], buf, axis=AX.X, op=ALU.add,
                            apply_absolute_value=True),
                            wait=p.now("syn"), inc=("vec", 1))
                yield absstep
            def meanstep(col=col):
                p.do("vector", lambda v: v.tensor_reduce(
                    r8[:, 64:65], r8[:, 0:64], axis=AX.X, op=ALU.add),
                    inc=("vec", 1))
                p.do("tensor", lambda t: t.matmul(
                    psPP[0:1, 0:1], r8[:, 64:65], onecol[:, :],
                    start=True, stop=True),
                    wait=p.now("vec"), inc=("ten", 1))
                p.do("vector", lambda v: v.tensor_scalar(
                    zpad[0:1, 0:1], psPP[0:1, 0:1], 1.0 / float(INNER * DM),
                    EPS, ALU.mult, ALU.max),
                    wait=p.now("ten"), inc=("vec", 1))
                p.do("tensor", lambda t: t.matmul(
                    psPP[:, 0:1], ones128[:, :], zpad[:, 0:1],
                    start=True, stop=True),
                    wait=p.now("vec"), inc=("ten", 1))
                p.do("vector", lambda v, col=col: v.tensor_copy(
                    wsc[:, col:col + 1], psPP[:, 0:1]), wait=p.now("ten"),
                    inc=("vec", 1))
                p.do("vector", lambda v, col=col: v.reciprocal(
                    swb[:, :], wsc[:, col:col + 1]), wait=p.now("vec"),
                    inc=("vec", 1))
            yield meanstep
            for piece in range(65):
                def qtstep(piece=piece, src3=src3, dst=dst):
                    if piece < 64:
                        buf = wraw[:, (piece % 2) * 512:(piece % 2) * 512 + 512]
                        p.do("sync", lambda s, buf=buf, piece=piece: s.dma_start(
                            out=buf, in_=src3[piece]),
                            wait=p.now("vec"), inc=("syn", 16))
                    if piece == 0:
                        return
                    piece = piece - 1
                    buf = wraw[:, (piece % 2) * 512:(piece % 2) * 512 + 512]
                    p.do("vector", lambda v, buf=buf: v.tensor_scalar(
                        wpos[:, :], buf, swb[:, :], 0.5, ALU.mult,
                        ALU.is_ge),
                        wait=p.now("syn") + p.now("gpd") + p.now("vec"))
                    p.do("vector", lambda v, buf=buf: v.tensor_scalar(
                        wneg[:, :], buf, swb[:, :], -0.5, ALU.mult,
                        ALU.is_le))
                    p.do("vector", lambda v: v.tensor_tensor(
                        wqn[:, :], wpos[:, :], wneg[:, :], op=ALU.subtract),
                        inc=("vec", 1), wait=p.now("ten"))
                    for b in range(4):
                        p.do("tensor", lambda t, b=b: t.matmul(
                            psPP[:, b * 128:(b + 1) * 128],
                            wqn[:, b * 128:(b + 1) * 128], ident[:, :],
                            start=True, stop=True),
                            wait=p.now("vec"),
                            inc=("ten", 1) if b == 3 else None)
                    p.do("vector", lambda v: v.tensor_copy(
                        wqt[:, :], psPP[:, :]),
                        wait=p.now("ten") + p.now("gpd"), inc=("vec", 1))
                    p.do("gpsimd", lambda g, dst=dst, piece=piece: g.dma_start(
                        out=dst[:, piece * 512:(piece + 1) * 512], in_=wqt[:, :]),
                        wait=p.now("vec"), inc=("gpd", 16))
                yield qtstep

    pp_list = list(make_prepass())
    pp_idx = [0]

    def run_prepass_steps(n):
        while n > 0 and pp_idx[0] < len(pp_list):
            pp_list[pp_idx[0]]()
            pp_idx[0] += 1
            n -= 1

    prev_exp = {0: 0, 1: 0}
    prev_av = [0, 0]
    av_evac_prev = 0
    wout_done = 0

    for qt in range(NTQ):
        q0 = qt * 512
        for hp in range(HC):
            kbuf = hp % 2
            kb = ksl[:, kbuf * SEQ:(kbuf + 1) * SEQ]
            ksl_guard = prev_exp[0]
            p.do("sync", lambda s, hp=hp, kb=kb: s.dma_start(
                out=kb, in_=scr_K[:, hp * SEQ:(hp + 1) * SEQ]),
                wait=[("sca", ksl_guard)] + list(a_end), inc=("syn", 16))
            syn_k = p.v("syn")
            if "D" in phases and qt == 0 and hp == 0:
                p.do("sync", lambda s, kb=kb: s.dma_start(out=dbg_k[:, :], in_=kb),
                     wait=p.now("vec"), inc=("syn", 16))
                p.do("sync", lambda s: s.dma_start(out=dbg_q[:, :], in_=qfm[:, :]),
                     inc=("syn", 16))
                syn_k = p.v("syn")
            exp_cnt = {}
            av_cnt = {}
            for kp in range(KP):
                kc0 = 2 * kp
                slot = kp % 2
                for half in range(2):
                    kcx = kc0 + half
                    p.do("tensor", lambda t, kb=kb, kcx=kcx, half=half, q0=q0,
                         hp=hp: t.matmul(
                        psS_A[:, half * 512:(half + 1) * 512],
                        kb[0:64, kcx * 128:(kcx + 1) * 128],
                        qfm[0:64, hp * OWN + q0: hp * OWN + q0 + 512],
                        start=True, stop=True, tile_position=(0, 0)),
                        wait=[("syn", syn_k),
                              ("sca", exp_cnt.get((kp - 1, 0), prev_exp[0]))])
                    p.do("tensor", lambda t, kb=kb, kcx=kcx, half=half, q0=q0,
                         hp=hp: t.matmul(
                        psS_B[:, half * 512:(half + 1) * 512],
                        kb[64:128, kcx * 128:(kcx + 1) * 128],
                        qfm[64:128, hp * OWN + q0: hp * OWN + q0 + 512],
                        start=True, stop=True, tile_position=(64, 0)),
                        wait=[("sca", exp_cnt.get((kp - 1, 1), prev_exp[1]))],
                        inc=("ten", 1) if half == 1 else None)
                sc_done = p.v("ten")
                for head, srcp in ((0, psS_A), (1, psS_B)):
                    p.do("scalar", lambda s, head=head, srcp=srcp, slot=slot:
                         s.activation(
                             expt[:, head * 2048 + slot * 1024:
                                  head * 2048 + slot * 1024 + 1024],
                             srcp[:, :], AF.Exp, scale=0.125),
                         wait=[("ten", sc_done),
                               ("ten", av_cnt.get(kp - 2, prev_av[slot]))],
                         inc=("sca", 1))
                    exp_cnt[(kp, head)] = p.v("sca")
                if "D" in phases and qt == 0 and hp == 0 and kp == 0:
                    p.do("sync", lambda s: s.dma_start(out=dbg_e[:, :], in_=expt[:, :]),
                         wait=[("sca", exp_cnt[(kp, 1)])], inc=("syn", 16))
                for (j, kcx) in ((0, kc0), (1, kc0 + 1)):
                    for head, pav in ((0, psAV_A), (1, psAV_B)):
                        base = kcx * VST + hp * PST
                        if head == 0:
                            lhs = vaug[:, base:base + 65]
                            outp = pav[0:65, :]
                        else:
                            lhs = vaug[:, base + 32:base + 160]
                            outp = pav[0:128, :]
                        p.do("tensor", lambda t, lhs=lhs, outp=outp, head=head,
                             slot=slot, j=j, kcx=kcx: t.matmul(
                            outp, lhs,
                            expt[:, head * 2048 + slot * 1024 + j * 512:
                                 head * 2048 + slot * 1024 + (j + 1) * 512],
                            start=(kcx == 0), stop=(kcx == KC - 1)),
                            wait=[("sca", exp_cnt[(kp, head)]),
                                  ("vec", av_evac_prev if kcx == 0 else 0)],
                            inc=("ten", 1) if (head == 1 and j == 1) else None)
                    av_cnt[kp] = p.v("ten")
            av_done = p.v("ten")
            prev_exp[0] = exp_cnt[(KP - 1, 0)]
            prev_exp[1] = exp_cnt[(KP - 1, 1)]
            prev_av = [av_cnt.get(KP - 2, av_done), av_cnt[KP - 1]]
            p.do("scalar", lambda s: s.activation(
                f32x[64:65, 0:512], psAV_A[64:65, :], AF.Ln),
                wait=[("ten", av_done), ("sca", p.v("sca"))]
                + (p.now("vec") if hp == 0 else []), inc=("sca", 1))
            p.do("scalar", lambda s: s.activation(
                zrow[64:65, 0:512], f32x[64:65, 0:512], AF.Exp, scale=-1.0),
                wait=[("sca", p.v("sca"))], inc=("sca", 1))
            p.do("scalar", lambda s: s.activation(
                f32x[32:33, 0:512], psAV_B[32:33, :], AF.Ln),
                wait=[("ten", av_done), ("sca", p.v("sca"))], inc=("sca", 1))
            p.do("scalar", lambda s: s.activation(
                zrow[32:33, 512:1024], f32x[32:33, 0:512], AF.Exp, scale=-1.0),
                wait=[("sca", p.v("sca"))], inc=("sca", 1))
            rec_done = p.v("sca")
            p.do("tensor", lambda t: t.matmul(
                psMSC[:, 0:512], ones128[:, :], zrow[:, 0:512],
                start=True, stop=True),
                wait=[("sca", rec_done)] + p.now("vec"), inc=("ten", 1))
            p.do("tensor", lambda t: t.matmul(
                psMSC[:, 512:1024], ones128[:, :], zrow[:, 512:1024],
                start=True, stop=True), inc=("ten", 1))
            bc_done = p.v("ten")
            p.do("vector", lambda v, hp=hp: v.tensor_copy(
                attn[0:64, hp * 512:(hp + 1) * 512], psAV_A[0:64, :]),
                wait=[("ten", bc_done), ("ten", wout_done)], inc=("vec", 1))
            p.do("vector", lambda v, hp=hp: v.tensor_tensor(
                attn[0:64, hp * 512:(hp + 1) * 512],
                attn[0:64, hp * 512:(hp + 1) * 512],
                psMSC[0:64, 0:512], op=ALU.mult), inc=("vec", 1))
            p.do("vector", lambda v, hp=hp: v.tensor_copy(
                attn[64:128, hp * 512:(hp + 1) * 512], psAV_B[64:128, :]),
                inc=("vec", 1))
            p.do("vector", lambda v, hp=hp: v.tensor_tensor(
                attn[64:128, hp * 512:(hp + 1) * 512],
                attn[64:128, hp * 512:(hp + 1) * 512],
                psMSC[64:128, 512:1024], op=ALU.mult), inc=("vec", 1))
            av_evac_prev = p.v("vec")
            run_prepass_steps(2)

        if "D" in phases and qt == 0:
            p.do("sync", lambda s: s.dma_start(out=dbg_q3[:, 0:HC*512], in_=attn[:, :]),
                 wait=p.now("vec", "sca"), inc=("syn", 16))
        attn_ready = p.v("vec")
        wo_evac = [0, 0]
        for oc in range(HC):
            par = oc % 2
            pw = psMSC[:, par * 512:par * 512 + 512]
            for hcb in range(HC):
                p.do("tensor", lambda t, oc=oc, hcb=hcb, pw=pw: t.matmul(
                    pw, wo_sb[:, hcb * DM + oc * 128: hcb * DM + (oc + 1) * 128],
                    attn[:, hcb * 512:(hcb + 1) * 512],
                    start=(hcb == 0), stop=(hcb == HC - 1)),
                    wait=[("vec", attn_ready), ("gpd", wo_ld),
                          ("vec", wo_evac[par] if hcb == 0 else 0)],
                    inc=("ten", 1) if hcb == HC - 1 else None)
            p.do("vector", lambda v, oc=oc, pw=pw: v.tensor_copy(
                ao_sb[:, oc * 512:(oc + 1) * 512], pw),
                wait=p.now("ten"), inc=("vec", 1))
            wo_evac[par] = p.v("vec")
        wout_done = p.v("ten")
        tr_evac = 0
        for s_i in range(4):
            for oc in range(HC):
                p.do("tensor", lambda t, s_i=s_i, oc=oc: t.matmul(
                    psMSC[:, oc * 128:(oc + 1) * 128],
                    ao_sb[:, oc * 512 + s_i * 128: oc * 512 + (s_i + 1) * 128],
                    ident[:, :], start=True, stop=True),
                    wait=[("vec", p.v("vec") if (s_i == 0 and oc == 0) else tr_evac)],
                    inc=("ten", 1) if oc == HC - 1 else None)
            p.do("vector", lambda v, s_i=s_i: v.tensor_copy(
                ao_tm[:, s_i * DM:(s_i + 1) * DM], psMSC[:, :]),
                wait=p.now("ten"), inc=("vec", 1))
            tr_evac = p.v("vec")
            xs = ao_tm[:, s_i * DM:(s_i + 1) * DM]
            p.do("vector", lambda v, xs=xs, s_i=s_i: v.tensor_reduce(
                st_sum[:, s_i:s_i + 1], xs, axis=AX.X, op=ALU.add))
            p.do("vector", lambda v, xs=xs: v.tensor_tensor(
                f32x[:, :], xs, xs, op=ALU.mult), wait=p.now("sca"))
            p.do("vector", lambda v, s_i=s_i: v.tensor_reduce(
                st_sq[:, s_i:s_i + 1], f32x[:, :], axis=AX.X, op=ALU.add),
                inc=("vec", 1))
        p.do("vector", lambda v: v.tensor_scalar(
            st_mean[:, 0:4], st_sum[:, 0:4], 1.0 / DM, None, ALU.mult),
            inc=("vec", 1))
        p.do("vector", lambda v: v.tensor_tensor(
            st_t[:, 0:4], st_mean[:, 0:4], st_mean[:, 0:4], op=ALU.mult),
            wait=p.now("vec"), inc=("vec", 1))
        p.do("vector", lambda v: v.scalar_tensor_tensor(
            st_var[:, 0:4], st_sq[:, 0:4], 1.0 / DM, st_t[:, 0:4],
            op0=ALU.mult, op1=ALU.subtract), wait=p.now("vec"), inc=("vec", 1))
        p.do("scalar", lambda s: s.activation(
            st_rstd[:, 0:4], st_var[:, 0:4], AF.Sqrt, bias=epsc[:, :], scale=1.0),
            wait=p.now("vec"), inc=("sca", 1))
        p.do("vector", lambda v: v.reciprocal(st_rstd[:, 0:4], st_rstd[:, 0:4]),
             wait=p.now("sca"), inc=("vec", 1))
        p.do("vector", lambda v: v.scalar_tensor_tensor(
            st_b[:, 0:4], st_mean[:, 0:4], -1.0, st_rstd[:, 0:4],
            op0=ALU.mult, op1=ALU.mult), wait=p.now("vec"), inc=("vec", 1))
        if "F" in phases and qt == 0:
            dq3f = dbg_q3.bitcast(F32)
            p.do("sync", lambda s: s.dma_start(
                out=dbg_q3[:, 4096:8192], in_=ao_sb[:, :]),
                wait=p.now("vec", "ten", "sca"), inc=("syn", 16))
            p.do("sync", lambda s: s.dma_start(
                out=dbg_q3[:, 8192:12288], in_=ao_tm[:, :]), inc=("syn", 16))
            p.do("sync", lambda s: s.dma_start(
                out=dq3f[:, 0:4], in_=st_mean[:, 0:4]), inc=("syn", 16))
            p.do("sync", lambda s: s.dma_start(
                out=dq3f[:, 4:8], in_=st_rstd[:, 0:4]), inc=("syn", 16))
            p.do("sync", lambda s: s.dma_start(
                out=dq3f[:, 8:12], in_=st_sq[:, 0:4]), inc=("syn", 16))
            p.do("sync", lambda s: s.dma_start(
                out=dq3f[:, 12:16], in_=st_var[:, 0:4]), inc=("syn", 16))
            p.do("sync", lambda s: s.dma_start(
                out=dq3f[:, 16:20], in_=st_sum[:, 0:4]), inc=("syn", 16))
            p.do("sync", lambda s: s.dma_start(
                out=dq3f[:, 20:24], in_=st_t[:, 0:4]), inc=("syn", 16))
            p.do("sync", lambda s: s.dma_start(
                out=dq3f[:, 1024:2048], in_=f32x[:, :]), inc=("syn", 16))
        x1_spill = 0
        h1_spill = 0
        h1tr_prev = [0]
        p.do("sync", lambda s, qt=qt: s.dma_start(
            out=xqt[:, 0:DM].unsqueeze(1),
            in_=xq_d[qt * 512:qt * 512 + 128, :].rearrange(
                "(a p) f -> p a f", p=128)),
            wait=p.now("vec"), inc=("syn", 16))
        syn_xq_next = p.v("syn")
        for s_i in range(4):
            sub = qt * 4 + s_i
            xqs = xqt[:, (s_i % 2) * DM:(s_i % 2) * DM + DM]
            syn_xq = syn_xq_next
            if s_i < 3:
                p.do("sync", lambda s, sub=sub, s_i=s_i: s.dma_start(
                    out=xqt[:, ((s_i + 1) % 2) * DM:((s_i + 1) % 2) * DM + DM]
                    .unsqueeze(1),
                    in_=xq_d[(sub + 1) * 128:(sub + 2) * 128, :].rearrange(
                        "(a p) f -> p a f", p=128)),
                    wait=p.now("vec"), inc=("syn", 16))
                syn_xq_next = p.v("syn")
            p.do("vector", lambda v, s_i=s_i: v.tensor_scalar(
                f32x[:, :], ao_tm[:, s_i * DM:(s_i + 1) * DM],
                st_mean[:, s_i:s_i + 1], st_rstd[:, s_i:s_i + 1],
                ALU.subtract, ALU.mult))
            p.do("vector", lambda v, xqs=xqs: v.tensor_tensor(
                x1t[:, :], f32x[:, :], xqs, op=ALU.add),
                wait=[("syn", syn_xq), ("gpd", x1_spill)], inc=("vec", 1))
            p.do("gpsimd", lambda g, sub=sub: g.dma_start(
                out=scr_x1[:, sub * DM:(sub + 1) * DM], in_=x1t[:, :]),
                wait=p.now("vec"), inc=("gpd", 16))
            x1_spill = p.v("gpd")
            p.do("vector", lambda v: v.tensor_reduce(
                st_sum[:, 0:1], x1t[:, :], axis=AX.X, op=ALU.add))
            p.do("vector", lambda v: v.tensor_tensor(
                f32x[:, :], x1t[:, :], x1t[:, :], op=ALU.mult))
            p.do("vector", lambda v: v.tensor_reduce(
                st_sq[:, 0:1], f32x[:, :], axis=AX.X, op=ALU.add))
            p.do("vector", lambda v: v.tensor_reduce(
                st_t[:, 1:2], x1t[:, :], axis=AX.X, op=ALU.max))
            p.do("vector", lambda v: v.tensor_reduce(
                st_t[:, 2:3], x1t[:, :], axis=AX.X, op=ALU.min))
            p.do("vector", lambda v: v.tensor_scalar(
                st_mean[:, 0:1], st_sum[:, 0:1], 1.0 / DM, None, ALU.mult),
                inc=("vec", 1))
            p.do("vector", lambda v: v.tensor_tensor(
                st_t[:, 0:1], st_mean[:, 0:1], st_mean[:, 0:1], op=ALU.mult),
                wait=p.now("vec"), inc=("vec", 1))
            p.do("vector", lambda v: v.scalar_tensor_tensor(
                st_var[:, 0:1], st_sq[:, 0:1], 1.0 / DM, st_t[:, 0:1],
                op0=ALU.mult, op1=ALU.subtract), wait=p.now("vec"),
                inc=("vec", 1))
            p.do("scalar", lambda s: s.activation(
                st_rstd[:, 0:1], st_var[:, 0:1], AF.Sqrt, bias=epsc[:, :],
                scale=1.0), wait=p.now("vec"), inc=("sca", 1))
            p.do("vector", lambda v: v.reciprocal(st_rstd[:, 0:1], st_rstd[:, 0:1]),
                 wait=p.now("sca"), inc=("vec", 1))
            p.do("vector", lambda v: v.tensor_tensor(
                st_t[:, 1:2], st_t[:, 1:2], st_mean[:, 0:1], op=ALU.subtract),
                inc=("vec", 1))
            p.do("vector", lambda v: v.tensor_tensor(
                st_t[:, 2:3], st_mean[:, 0:1], st_t[:, 2:3], op=ALU.subtract),
                inc=("vec", 1))
            p.do("vector", lambda v: v.tensor_tensor(
                st_t[:, 1:2], st_t[:, 1:2], st_t[:, 2:3], op=ALU.max),
                wait=p.now("vec"), inc=("vec", 1))
            p.do("vector", lambda v: v.tensor_tensor(
                st_t[:, 1:2], st_t[:, 1:2], st_rstd[:, 0:1], op=ALU.mult),
                wait=p.now("vec"), inc=("vec", 1))
            p.do("vector", lambda v: v.tensor_scalar(
                st_t[:, 1:2], st_t[:, 1:2], EPS, None, ALU.max),
                wait=p.now("vec"), inc=("vec", 1))
            p.do("vector", lambda v, sub=sub: v.tensor_scalar(
                c1sc[:, sub:sub + 1], st_t[:, 1:2], 1.0 / 127.0, None, ALU.mult),
                wait=p.now("vec"), inc=("vec", 1))
            p.do("vector", lambda v: v.reciprocal(st_t[:, 2:3], st_t[:, 1:2]),
                 inc=("vec", 1))
            p.do("vector", lambda v: v.tensor_scalar(
                st_t[:, 2:3], st_t[:, 2:3], 127.0, None, ALU.mult),
                wait=p.now("vec"), inc=("vec", 1))
            p.do("vector", lambda v: v.tensor_tensor(
                st_a[:, 0:1], st_rstd[:, 0:1], st_t[:, 2:3], op=ALU.mult),
                wait=p.now("vec"), inc=("vec", 1))
            p.do("vector", lambda v: v.scalar_tensor_tensor(
                st_b[:, 0:1], st_mean[:, 0:1], -1.0, st_a[:, 0:1],
                op0=ALU.mult, op1=ALU.mult), wait=p.now("vec"), inc=("vec", 1))
            p.do("vector", lambda v: v.tensor_scalar(
                f32x[:, :], x1t[:, :], st_a[:, 0:1], st_b[:, 0:1],
                ALU.mult, ALU.add), wait=p.now("vec"))
            p.do("vector", lambda v: v.tensor_scalar(
                sqd[:, :], f32x[:, :], MAGIC, MAGIC, ALU.add, ALU.subtract),
                wait=[("ten", h1tr_prev[0])], inc=("vec", 1))
            hq_cnt = p.v("vec")
            for hcb in range(HC):
                p.do("tensor", lambda t, hcb=hcb: t.matmul(
                    psMSC[:, hcb * 128:(hcb + 1) * 128],
                    sqd[:, hcb * 128:(hcb + 1) * 128], ident[:, :],
                    start=True, stop=True),
                    wait=[("vec", hq_cnt)],
                    inc=("ten", 1) if hcb == HC - 1 else None)
            h1tr_prev[0] = p.v("ten")
            p.do("vector", lambda v: v.tensor_copy(h1qst[:, :], psMSC[:, :]),
                 wait=p.now("ten") + p.now("gpd"), inc=("vec", 1))
            for hcb in range(HC):
                p.do("gpsimd", lambda g, sub=sub, hcb=hcb: g.dma_start(
                    out=scr_h1q[:, hcb * OWN + sub * 128:
                                hcb * OWN + (sub + 1) * 128],
                    in_=h1qst[:, hcb * 128:(hcb + 1) * 128]),
                    wait=p.now("vec"), inc=("gpd", 16))
            h1_spill = p.v("gpd")
        run_prepass_steps(4)

    run_prepass_steps(len(pp_list))
    b_done = p.now("gps", "gpd", "ten", "vec", "sca", "syn")

    ar.release(mark_AB)

    if "C" not in phases:
        fin0 = p.v("syn")
        p.do("sync", lambda s: s.nop(),
             wait=[("syn", fin0), ("ten", p.v("ten")), ("vec", p.v("vec")),
                   ("sca", p.v("sca")), ("gpd", p.v("gpd"))])
        return _emit(nc, p, es)
    mark_C = ar.mark()
    sbC = ar.sb
    psC = ar.ps

    wgT = sbC("wgT", [128, IC * DM], BF16)
    wuT = sbC("wuT", [128, IC * DM], BF16)
    wdToc = sbC("wdToc", [128, INNER], BF16)
    gqfm = sbC("gqfm", [128, IC * 512], BF16)
    uscr = sbC("uscr", [128, INNER])
    gqtm = sbC("gqtm", [128, INNER], BF16)
    sgsb = sbC("sgsb", [128, 512])
    h1qg = sbC("h1qg", [128, HC * 128], BF16)
    ffntm = sbC("ffntm", [128, DM], BF16)

    psG = psC("psG", [128, 1024])
    psU = psC("psU", [128, 1024])
    psT = psC("psT", [128, 1024])
    psD = psC("psD", [128, 1024])

    do_sb = gqtm
    x1s = uscr[:, 0:DM]
    f32c = uscr[:, DM:2 * DM]
    outst = uscr[:, 2 * DM:3 * DM]

    p.do("sync", lambda s: s.dma_start(out=wgT[:, :], in_=scr_wgT[:, :]),
         wait=b_done, inc=("syn", 16))
    p.do("sync", lambda s: s.dma_start(out=wuT[:, :], in_=scr_wuT[:, :]),
         inc=("syn", 16))
    wgu_ld = p.v("syn")

    def wT_rhs(w, hcb, n512):
        img = w.rearrange("p (i h f) -> p i h f", i=IC, h=HC)
        return img[:, 4 * n512:4 * n512 + 4, hcb, :]

    for grp in range(NGRP):
        g0 = grp * 4
        gu_evac = [p.v("sca")] * 2
        gu_vec = [p.v("vec")] * 2
        gqtr_prev = [0]
        for ti in range(4):
            sub = g0 + ti
            for hcb in range(HC):
                p.do("sync", lambda s, sub=sub, hcb=hcb: s.dma_start(
                    out=h1qg[:, hcb * 128:(hcb + 1) * 128],
                    in_=scr_h1q[:, hcb * OWN + sub * 128:
                                hcb * OWN + sub * 128 + 128]),
                    wait=[("syn", wgu_ld), ("ten", p.v("ten")),
                          ("vec", p.v("vec"))], inc=("syn", 16))
            h1_ld = p.v("syn")
            p.do("vector", lambda v, sub=sub: v.tensor_tensor(
                st_a[:, 0:1], c1sc[:, sub:sub + 1], wsc[:, 0:1], op=ALU.mult),
                wait=[("syn", h1_ld)] + p.now("sca"))
            p.do("vector", lambda v, sub=sub: v.tensor_tensor(
                st_a[:, 1:2], c1sc[:, sub:sub + 1], wsc[:, 1:2], op=ALU.mult),
                inc=("vec", 1))
            cg_cnt = p.v("vec")
            for n512 in range(HC):
                par = n512 % 2
                pg = psG[:, par * 512:par * 512 + 512]
                pu = psU[:, par * 512:par * 512 + 512]
                for hcb in range(HC):
                    p.do("tensor", lambda t, hcb=hcb, n512=n512, ti=ti, pg=pg:
                         t.matmul(
                             pg, h1qg[:, hcb * 128:(hcb + 1) * 128],
                             wT_rhs(wgT, hcb, n512),
                             start=(hcb == 0), stop=(hcb == HC - 1)),
                         wait=[("syn", h1_ld),
                               ("sca", gu_evac[par] if hcb == 0 else 0)],
                         inc=("ten", 1) if hcb == HC - 1 else None)
                for hcb in range(HC):
                    p.do("tensor", lambda t, hcb=hcb, n512=n512, ti=ti, pu=pu:
                         t.matmul(
                             pu, h1qg[:, hcb * 128:(hcb + 1) * 128],
                             wT_rhs(wuT, hcb, n512),
                             start=(hcb == 0), stop=(hcb == HC - 1)),
                         wait=[("vec", gu_vec[par] if hcb == 0 else 0)],
                         inc=("ten", 1) if hcb == HC - 1 else None)
                gu_cnt = p.v("ten")
                p.do("scalar", lambda s, pg=pg: s.activation(
                    sgsb[:, :], pg, AF.Silu, scale=st_a[:, 0:1]),
                    wait=[("ten", gu_cnt), ("vec", cg_cnt)] + p.now("vec"),
                    inc=("sca", 1))
                p.do("vector", lambda v, pu=pu, n512=n512: v.scalar_tensor_tensor(
                    uscr[:, n512 * 512:(n512 + 1) * 512], pu, st_a[:, 1:2],
                    sgsb[:, :], op0=ALU.mult, op1=ALU.mult),
                    wait=p.now("sca"), inc=("vec", 1))
                gu_evac[par] = p.v("sca")
                gu_vec[par] = p.v("vec")
            g_cnt = p.v("vec")
            p.do("vector", lambda v: v.tensor_reduce(
                st_t[:, 0:1], uscr[:, :], axis=AX.X, op=ALU.max,
                apply_absolute_value=True), wait=[("vec", g_cnt)],
                inc=("vec", 1))
            p.do("vector", lambda v: v.tensor_scalar(
                st_t[:, 0:1], st_t[:, 0:1], EPS, None, ALU.max),
                wait=p.now("vec"), inc=("vec", 1))
            p.do("vector", lambda v, sub=sub: v.scalar_tensor_tensor(
                cdwsc[:, sub:sub + 1], st_t[:, 0:1], 1.0 / 127.0, wsc[:, 2:3],
                op0=ALU.mult, op1=ALU.mult), wait=p.now("vec"), inc=("vec", 1))
            p.do("vector", lambda v: v.reciprocal(st_t[:, 1:2], st_t[:, 0:1]),
                 inc=("vec", 1))
            p.do("vector", lambda v: v.tensor_scalar(
                st_t[:, 1:2], st_t[:, 1:2], 127.0, None, ALU.mult),
                wait=p.now("vec"), inc=("vec", 1))
            p.do("vector", lambda v: v.tensor_scalar(
                uscr[:, :], uscr[:, :], st_t[:, 1:2], MAGIC, ALU.mult, ALU.add),
                wait=p.now("vec"))
            p.do("vector", lambda v: v.tensor_scalar(
                gqtm[:, :], uscr[:, :], MAGIC, None, ALU.subtract),
                wait=[("ten", gqtr_prev[0])], inc=("vec", 1))
            rq_cnt = p.v("vec")
            if "G" in phases and grp == 1 and ti == 0:
                dq2f = dbg_q2.bitcast(F32)
                p.do("sync", lambda s: s.dma_start(
                    out=dq2f[:, 0:4096], in_=uscr[:, :]), wait=p.now("vec"),
                    inc=("syn", 16))
                p.do("sync", lambda s: s.dma_start(
                    out=dq2f[:, 4096:4100], in_=st_t[:, 0:4],
                    allow_slow_non_contiguous=True), inc=("syn", 16))
                p.do("sync", lambda s, sub=sub: s.dma_start(
                    out=dq2f[:, 4100:4101], in_=cdwsc[:, sub:sub + 1],
                    allow_slow_non_contiguous=True), inc=("syn", 16))
                _dg = p.v("syn")
                p.do("vector", lambda v: v.nop(), wait=[("syn", _dg)],
                     inc=("vec", 1))
            gq3 = gqfm[:, :].rearrange("p (i t) -> p i t", i=IC)
            tr_ev = 0
            for fill in range(4):
                for j in range(HC):
                    icb = fill * 8 + j
                    p.do("tensor", lambda t, icb=icb, j=j: t.matmul(
                        psT[:, j * 128:(j + 1) * 128],
                        gqtm[:, icb * 128:(icb + 1) * 128],
                        ident[:, :], start=True, stop=True),
                        wait=[("vec", rq_cnt if fill == 0 else tr_ev)],
                        inc=("ten", 1) if j == HC - 1 else None)
                p.do("vector", lambda v, fill=fill, ti=ti, gq3=gq3: v.tensor_copy(
                    gq3[:, fill * 8:(fill + 1) * 8, ti * 128:(ti + 1) * 128],
                    psT[:, :].rearrange("p (i f) -> p i f", i=HC)),
                    wait=p.now("ten"), inc=("vec", 1))
                tr_ev = p.v("vec")
                gqtr_prev[0] = p.v("ten")
        if "G" in phases and grp == 1:
            p.do("sync", lambda s: s.dma_start(
                out=dbg_q3[:, :], in_=gqfm[:, :]),
                wait=p.now("vec"), inc=("syn", 16))
            _dg2 = p.v("syn")
            p.do("tensor", lambda t: t.nop(), wait=[("syn", _dg2)],
                 inc=("ten", 1))
        gq_ready = p.v("vec")
        do_evac = [p.v("vec")] * 2
        for oc in range(HC):
            p.do("sync", lambda s, oc=oc: s.dma_start(
                out=wdToc[:, :], in_=scr_wdT[:, oc * INNER:(oc + 1) * INNER]),
                wait=p.now("ten"), inc=("syn", 16))
            wd_ld = p.v("syn")
            par = oc % 2
            pd = psD[:, par * 512:par * 512 + 512]
            for icb in range(IC):
                p.do("tensor", lambda t, icb=icb, pd=pd: t.matmul(
                    pd, wdToc[:, icb * 128:(icb + 1) * 128],
                    gqfm[:, icb * 512:(icb + 1) * 512],
                    start=(icb == 0), stop=(icb == IC - 1)),
                    wait=[("syn", wd_ld), ("vec", gq_ready),
                          ("vec", do_evac[par] if icb == 0 else 0)],
                    inc=("ten", 1) if icb == IC - 1 else None)
            p.do("vector", lambda v, oc=oc, pd=pd: v.tensor_copy(
                do_sb[:, oc * 512:(oc + 1) * 512], pd),
                wait=p.now("ten"), inc=("vec", 1))
            do_evac[par] = p.v("vec")
        ftr_ev = p.v("vec")
        out_prev = [0]
        for ti in range(4):
            sub = g0 + ti
            for oc in range(HC):
                p.do("tensor", lambda t, oc=oc, ti=ti: t.matmul(
                    psT[:, oc * 128:(oc + 1) * 128],
                    do_sb[:, oc * 512 + ti * 128: oc * 512 + (ti + 1) * 128],
                    ident[:, :], start=True, stop=True),
                    wait=[("vec", ftr_ev)],
                    inc=("ten", 1) if oc == HC - 1 else None)
            p.do("vector", lambda v: v.tensor_copy(ffntm[:, :], psT[:, :]),
                 wait=p.now("ten"), inc=("vec", 1))
            ftr_ev = p.v("vec")
            p.do("sync", lambda s, sub=sub: s.dma_start(
                out=x1s, in_=scr_x1[:, sub * DM:(sub + 1) * DM]),
                wait=[("vec", ftr_ev)], inc=("syn", 16))
            syn_x1 = p.v("syn")
            p.do("vector", lambda v: v.tensor_reduce(
                st_sum[:, 0:1], ffntm[:, :], axis=AX.X, op=ALU.add))
            p.do("vector", lambda v: v.tensor_tensor(
                f32c, ffntm[:, :], ffntm[:, :], op=ALU.mult))
            p.do("vector", lambda v: v.tensor_reduce(
                st_sq[:, 0:1], f32c, axis=AX.X, op=ALU.add))
            p.do("vector", lambda v, sub=sub: v.scalar_tensor_tensor(
                st_mean[:, 0:1], st_sum[:, 0:1], 1.0 / DM, cdwsc[:, sub:sub + 1],
                op0=ALU.mult, op1=ALU.mult), inc=("vec", 1))
            p.do("vector", lambda v, sub=sub: v.tensor_tensor(
                st_t[:, 0:1], cdwsc[:, sub:sub + 1], cdwsc[:, sub:sub + 1],
                op=ALU.mult), inc=("vec", 1))
            p.do("vector", lambda v: v.tensor_tensor(
                st_t[:, 0:1], st_t[:, 0:1], st_sq[:, 0:1], op=ALU.mult),
                wait=p.now("vec"), inc=("vec", 1))
            p.do("vector", lambda v: v.tensor_tensor(
                st_t2[:, 0:1], st_mean[:, 0:1], st_mean[:, 0:1], op=ALU.mult),
                wait=p.now("vec"), inc=("vec", 1))
            p.do("vector", lambda v: v.scalar_tensor_tensor(
                st_var[:, 0:1], st_t[:, 0:1], 1.0 / DM, st_t2[:, 0:1],
                op0=ALU.mult, op1=ALU.subtract), wait=p.now("vec"),
                inc=("vec", 1))
            p.do("scalar", lambda s: s.activation(
                st_rstd[:, 0:1], st_var[:, 0:1], AF.Sqrt, bias=epsc[:, :],
                scale=1.0), wait=p.now("vec"), inc=("sca", 1))
            p.do("vector", lambda v: v.reciprocal(st_rstd[:, 0:1], st_rstd[:, 0:1]),
                 wait=p.now("sca"), inc=("vec", 1))
            p.do("vector", lambda v, sub=sub: v.tensor_tensor(
                st_a[:, 0:1], cdwsc[:, sub:sub + 1], st_rstd[:, 0:1], op=ALU.mult),
                wait=p.now("vec"), inc=("vec", 1))
            p.do("vector", lambda v: v.scalar_tensor_tensor(
                st_b[:, 0:1], st_mean[:, 0:1], -1.0, st_rstd[:, 0:1],
                op0=ALU.mult, op1=ALU.mult), wait=p.now("vec"), inc=("vec", 1))
            p.do("vector", lambda v: v.tensor_scalar(
                f32c, ffntm[:, :], st_a[:, 0:1], st_b[:, 0:1],
                ALU.mult, ALU.add), wait=[("syn", syn_x1)] + p.now("vec"))
            p.do("vector", lambda v: v.tensor_tensor(
                outst, f32c, x1s, op=ALU.add), inc=("vec", 1))
            p.do("sync", lambda s, sub=sub: s.dma_start(
                out=out_d[sub * 128:(sub + 1) * 128, :].rearrange(
                    "(a p) f -> p a f", p=128),
                in_=outst.unsqueeze(1)),
                wait=p.now("vec"), inc=("syn", 16))
            out_prev[0] = p.v("syn")

    for i in range(6):
        p.do("sync", lambda s: s.dma_start(
            out=gqtm[:, 0:128].bitcast(F32)[:, 0:64], in_=out_d[0:128, 0:64]),
            wait=p.now("syn"), inc=("syn", 16))
    fin = p.v("syn")
    p.do("sync", lambda s: s.nop(), wait=[("syn", fin)])

    ar.release(mark_C)
    return _emit(nc, p, es)


def _emit(nc, p, es):
    with nc.Block() as block:
        @block.sync
        def _(e):
            for fn in p.ops["sync"]:
                fn(e)

        @block.gpsimd
        def _(e):
            for fn in p.ops["gpsimd"]:
                fn(e)

        @block.scalar
        def _(e):
            for fn in p.ops["scalar"]:
                fn(e)

        @block.vector
        def _(e):
            for fn in p.ops["vector"]:
                fn(e)

        @block.tensor
        def _(e):
            for fn in p.ops["tensor"]:
                fn(e)

    es.close()
    return nc


_NC_CACHE = {}


def _get_nc(seq):
    if seq not in _NC_CACHE:
        _NC_CACHE[seq] = build_nc(seq)
    return _NC_CACHE[seq]


def kernel(L0_attn_X, L0_attn_W_qkv, L0_attn_W_out, L0_ffn_w_gate,
           L0_ffn_w_up, L0_ffn_w_down):
    X = np.ascontiguousarray(np.asarray(L0_attn_X, dtype=np.float32))
    Wqkv = np.asarray(L0_attn_W_qkv, dtype=np.float32)
    B, S, _ = X.shape
    OWN = S // 2
    idx = np.arange(3 * DM).reshape(NH, DH, 3)
    wq = np.ascontiguousarray(Wqkv[:, idx[:, :, 0].reshape(-1)])
    wk = np.ascontiguousarray(Wqkv[:, idx[:, :, 1].reshape(-1)])
    wv = np.ascontiguousarray(Wqkv[:, idx[:, :, 2].reshape(-1)])
    wo = np.ascontiguousarray(np.asarray(L0_attn_W_out, dtype=np.float32))
    wg = np.ascontiguousarray(np.asarray(L0_ffn_w_gate, dtype=np.float32))
    wu = np.ascontiguousarray(np.asarray(L0_ffn_w_up, dtype=np.float32))
    wd = np.ascontiguousarray(np.asarray(L0_ffn_w_down, dtype=np.float32))

    nc = _get_nc(S)
    in_maps = []
    ncore = 2 * B
    for c in range(ncore):
        b, h = c // 2, c % 2
        in_maps.append(dict(
            xb=X[b], xq=np.ascontiguousarray(X[b, h * OWN:(h + 1) * OWN]),
            wq=wq, wk=wk, wv=wv, wo=wo, wg=wg, wu=wu, wd=wd))
    res = run_bass_kernel_spmd(nc, in_maps, core_ids=list(range(ncore)))
    out = np.empty((B, S, DM), np.float32)
    for c in range(ncore):
        b, h = c // 2, c % 2
        out[b, h * OWN:(h + 1) * OWN] = res.results[c]["out"]
    return out
